# version 04
# baseline (speedup 1.0000x reference)
import contextlib
import numpy as np
import ml_dtypes
import concourse.bass as bass
import concourse.mybir as mybir
from concourse.bass_utils import run_bass_kernel_spmd

F32 = mybir.dt.float32
BF16 = mybir.dt.bfloat16
AF = mybir.ActivationFunctionType
ALU = mybir.AluOpType
AX = mybir.AxisListType

NCORES = 8
T = 2048
D = 1024
NDT = D // 128
NTC = T // 512
NH = 16
DH = 64
DI = 2048
NIT = DI // 128
DST = 16
DTR = 64
NE = 8
DFFE = 8192
NEG = -3.0e38
EPS = 1e-5

STAGES_ALL = ("A", "M1", "B", "M2")


class Sem:
    def __init__(self, h):
        self.h = h
        self.n = 0


def host_constants():
    c = {}
    c["ident_f32"] = np.eye(128, dtype=np.float32)
    c["ones_1x128"] = np.ones((1, 128), dtype=np.float32)
    c["ones_1x128_s16"] = np.full((1, 128), 1.0 / 16.0, dtype=np.float32)
    c["ones_128x1"] = np.ones((128, 1), dtype=np.float32)
    cv = np.zeros((128, 8), dtype=np.float32)
    cv[:, 0] = EPS
    cv[:, 1] = 1.0
    c["cvals"] = cv
    p = np.arange(128)[:, None]
    f = np.arange(512)[None, :]
    mb = np.zeros((128, 4, 512), dtype=np.float32)
    dm = np.zeros((128, 4, 512), dtype=np.float32)
    for r in range(4):
        mb[:, r, :] = np.where(f >= 128 * r + p, 0.0, NEG)
        dm[:, r, :] = np.where(f == 128 * r + p, 1.0, 0.0)
    c["maskdiag"] = np.concatenate([mb.reshape(128, 2048), dm.reshape(128, 2048)],
                                   axis=1)
    return c


def build_program(stages=STAGES_ALL):
    nc = bass.Bass("TRN2", target_bir_lowering=False)
    es = contextlib.ExitStack()

    inp = {}

    def dri(name, shape):
        inp[name] = nc.dram_tensor(name, shape, F32, kind="ExternalInput")
        return inp[name]

    x_h = dri("xcore", [T, D])
    wq_h = dri("aWQ", [NH, D, DH])
    wk_h = dri("aWK", [NH, D, DH])
    wvd_h = dri("aWVd", [NH, D, DH])
    wvu_h = dri("aWVu", [NH, DH, D])
    for m in ("m1", "m2"):
        dri(f"b_{m}_inwt", [16, 128, NDT, 256])
        dri(f"b_{m}_outT", [DI, D])
        dri(f"b_{m}_xprojT", [DI, DTR + 2 * DST])
        dri(f"b_{m}_dtT", [DTR, DI])
        dri(f"b_{m}_cw", [NIT, 128, 4])
        dri(f"b_{m}_cb", [128, NIT])
        dri(f"b_{m}_dtb", [128, NIT])
        dri(f"b_{m}_Alog", [NIT, 128, DST])
        dri(f"b_{m}_D", [128, NIT])
    dri("b_ng", [128, NDT])
    dri("b_nb", [128, NDT])
    dri("b_fg", [128, NDT])
    dri("b_fb", [128, NDT])
    dri("b_ff1t", [32, 128, NDT, 128])
    dri("b_ff1b", [128, 32])
    dri("b_ff2T", [4 * D, D])
    dri("b_ff2b", [128, 8])
    dri("moe_w1x", [NE, 32, 128, NDT, 128])
    dri("moe_w1g", [NE, 32, 128, NDT, 128])
    dri("moe_w2t", [NE, 16, 128, 2, D])
    dri("moe_b1c", [NE, 128, 64])
    dri("moe_b2c", [NE, 128, 8])
    dri("moe_gwr", [NDT, 128, NE])
    dri("moe_gbc", [NE, 1])

    hc = host_constants()
    for k, v in hc.items():
        inp[k] = nc.dram_tensor(k, list(v.shape), F32, kind="ExternalInput")

    out_h = nc.dram_tensor("out", [T, D], F32, kind="ExternalOutput")
    flat_hb = nc.dram_tensor("flat_hb", [1, 4096], F32, kind="Internal")
    coef_hb = nc.dram_tensor("coef_hb", [8, 1024], F32, kind="Internal")
    xspill = nc.dram_tensor("xspill", [128, NDT * T], F32, kind="Internal")

    def sbuf(name, shape, dtype):
        return es.enter_context(nc.sbuf_tensor(name, shape, dtype))

    B1 = sbuf("B1", [128, 16384], BF16)
    B2 = sbuf("B2", [128, 8192], BF16)
    B3 = sbuf("B3", [128, 8192], BF16)
    B4 = sbuf("B4", [128, 8192], BF16)
    B5 = sbuf("B5", [128, 8192], BF16)
    W5 = sbuf("W5", [128, 5632], BF16)
    F1 = sbuf("F1", [128, 1024], F32)
    F3 = sbuf("F3", [128, 1024], F32)
    vec = sbuf("vec", [1, 4096], F32)
    vec2 = sbuf("vec2", [1, 2048], F32)
    sm1 = sbuf("sm1", [128, 40], F32)
    flat16 = sbuf("flat16", [16, 128], F32)
    cI = sbuf("cI", [128, 128], F32)
    c1x = sbuf("c1x", [1, 128], F32)
    c1x16 = sbuf("c1x16", [1, 128], F32)
    cCol = sbuf("cCol", [128, 1], F32)
    cVal = sbuf("cVal", [128, 8], F32)
    gbc = sbuf("gbc", [8, 1], F32)
    b1sb = sbuf("b1sb", [128, 64], F32)
    b2sb = sbuf("b2sb", [128, 8], F32)
    ngc = sbuf("ngc", [128, 8], F32)
    nbc2 = sbuf("nbc2", [128, 8], F32)
    fgc = sbuf("fgc", [128, 8], F32)
    fbc = sbuf("fbc", [128, 8], F32)
    f1bsb = sbuf("f1bsb", [128, 32], F32)
    f2bsb = sbuf("f2bsb", [128, 8], F32)
    cColbf = sbuf("cColbf", [128, 1], BF16)
    c1xbf = sbuf("c1xbf", [1, 128], BF16)

    es_x = contextlib.ExitStack()
    X = {}

    def alloc_x(tag=""):
        X["sb"] = es_x.enter_context(nc.sbuf_tensor("x" + tag, [128, NDT * T], F32))
        X["v"] = X["sb"][:].rearrange("p (dt t) -> p dt t", dt=NDT)

    alloc_x()

    psf = [es.enter_context(nc.psum_tensor(f"psf{i}", [128, 512], F32))
           for i in range(8)]

    sq = Sem(es.enter_context(nc.semaphore("sq")))
    gq = Sem(es.enter_context(nc.semaphore("gq")))
    st_ = Sem(es.enter_context(nc.semaphore("st")))
    sv = Sem(es.enter_context(nc.semaphore("sv")))
    ss = Sem(es.enter_context(nc.semaphore("ss")))
    ph = Sem(es.enter_context(nc.semaphore("ph")))
    nbar = [0]

    nbv = B1[:].rearrange("p (dt t) -> p dt t", dt=NDT)

    with nc.Block() as block:

        def barrier():
            nbar[0] += 1
            tgt = 5 * nbar[0]
            sqn, gqn, stn, svn, ssn = sq.n, gq.n, st_.n, sv.n, ss.n

            @block.tensor
            def _(t):
                t.wait_ge(st_.h, stn)
                t.sem_inc(ph.h, 1)
                t.wait_ge(ph.h, tgt)

            @block.vector
            def _(v):
                v.wait_ge(sv.h, svn)
                v.sem_inc(ph.h, 1)
                v.wait_ge(ph.h, tgt)

            @block.scalar
            def _(s):
                s.wait_ge(ss.h, ssn)
                s.sem_inc(ph.h, 1)
                s.wait_ge(ph.h, tgt)

            @block.gpsimd
            def _(g):
                g.wait_ge(gq.h, gqn)
                g.sem_inc(ph.h, 1)
                g.wait_ge(ph.h, tgt)

            @block.sync
            def _(q):
                q.wait_ge(sq.h, sqn)
                q.sem_inc(ph.h, 1)
                q.wait_ge(ph.h, tgt)

        def phase_load():
            @block.sync
            def _(q):
                q.dma_start(out=cI[:], in_=inp["ident_f32"].ap()).then_inc(sq.h, 16)
                q.dma_start(out=c1x[:], in_=inp["ones_1x128"].ap()).then_inc(sq.h, 16)
                q.dma_start(out=c1x16[:], in_=inp["ones_1x128_s16"].ap()).then_inc(sq.h, 16)
                q.dma_start(out=cCol[:], in_=inp["ones_128x1"].ap()).then_inc(sq.h, 16)
                q.dma_start(out=cVal[:], in_=inp["cvals"].ap()).then_inc(sq.h, 16)
                q.dma_start(out=gbc[:], in_=inp["moe_gbc"].ap()).then_inc(sq.h, 16)
            sq.n += 96

            @block.gpsimd
            def _(g):
                g.dma_start(out=cColbf[:],
                            in_=inp["ones_128x1"].ap()).then_inc(sq.h, 16)
                g.dma_start(out=c1xbf[:],
                            in_=inp["ones_1x128"].ap()).then_inc(sq.h, 16)
            sq.n += 32
            consts_ld = sq.n

            xh = x_h.ap().rearrange("(tt p) d -> tt p d", p=128)
            tr_done = {}
            cp_done = {}
            for tt in range(16):
                land = F1 if tt % 2 == 0 else F3

                @block.gpsimd
                def _(g, tt=tt, land=land):
                    if tt >= 2:
                        g.wait_ge(st_.h, tr_done[tt - 2])
                    g.dma_start(out=land[:, 0:1024], in_=xh[tt]).then_inc(gq.h, 16)
                gq.n += 16

                @block.tensor
                def _(t, tt=tt, land=land, gqn=gq.n):
                    t.wait_ge(gq.h, gqn)
                    if tt == 0:
                        t.wait_ge(sq.h, consts_ld)
                    if tt >= 1:
                        t.wait_ge(sv.h, cp_done[tt - 1])
                    for dt in range(NDT):
                        t.transpose(psf[dt][:, 0:128],
                                    land[:, dt * 128:(dt + 1) * 128],
                                    cI[:]).then_inc(st_.h, 1)
                st_.n += NDT
                tr_done[tt] = st_.n

                @block.vector
                def _(v, tt=tt, base=st_.n - NDT):
                    for dt in range(NDT):
                        v.wait_ge(st_.h, base + dt + 1)
                        v.tensor_copy(X["v"][:, dt, tt * 128:(tt + 1) * 128],
                                      psf[dt][:, 0:128]).then_inc(sv.h, 1)
                sv.n += NDT
                cp_done[tt] = sv.n

            barrier()

        def emit_norm(src_view, sl, write_nb=True, affine=None,
                      fold_residual=False, dst_view=None, ones_col=None):
            if dst_view is None:
                dst_view = nbv
            oc = cCol if ones_col is None else ones_col
            vmu, vvar, vtmp = sl["mu"], sl["var"], sl["tmp"]
            vaB, vbB = sl["aB"], sl["bB"]
            vaB1 = sl.get("aB1")
            stats_done = {}
            cps = {}
            for tch in range(NTC):
                for qr in range(4):
                    @block.scalar
                    def _(s, tch=tch, qr=qr):
                        prev = (tch, qr - 1) if qr else (tch - 1, 3)
                        if prev in stats_done:
                            s.wait_ge(st_.h, stats_done[prev])
                        for k in range(2):
                            dt = qr * 2 + k
                            ins = s.activation(F1[:, k * 512:(k + 1) * 512],
                                               src_view[:, dt, tch * 512:(tch + 1) * 512],
                                               AF.Square)
                        ins.then_inc(ss.h, 1)
                    ss.n += 1

                    @block.tensor
                    def _(t, tch=tch, qr=qr, ssn=ss.n):
                        t.wait_ge(ss.h, ssn)
                        if qr == 0 and tch >= 1:
                            t.wait_ge(sv.h, cps[tch - 1])
                        for k in range(2):
                            dt = qr * 2 + k
                            t.matmul(psf[4][0:1, :], oc[:],
                                     src_view[:, dt, tch * 512:(tch + 1) * 512],
                                     start=(dt == 0), stop=(dt == NDT - 1))
                            ins = t.matmul(psf[5][0:1, :], cCol[:],
                                           F1[:, k * 512:(k + 1) * 512],
                                           start=(dt == 0), stop=(dt == NDT - 1))
                        ins.then_inc(st_.h, 1)
                    st_.n += 1
                    stats_done[(tch, qr)] = st_.n

                @block.vector
                def _(v, tch=tch, stn=st_.n):
                    v.wait_ge(st_.h, stn)
                    o = tch * 512
                    v.tensor_scalar_mul(vmu[0:1, o:o + 512], psf[4][0:1, :], 1.0 / D)
                    v.tensor_copy(vvar[0:1, o:o + 512],
                                  psf[5][0:1, :]).then_inc(sv.h, 1)
                sv.n += 1
                cps[tch] = sv.n

            @block.vector
            def _(v):
                v.tensor_mul(vtmp, vmu, vmu)
                v.scalar_tensor_tensor(vvar, vvar, 1.0 / D, vtmp,
                                       ALU.mult, ALU.subtract).then_inc(sv.h, 1)
            sv.n += 1

            @block.scalar
            def _(s, svn=sv.n):
                s.wait_ge(sv.h, svn)
                s.activation(vtmp, vvar, AF.Ln, bias=cVal[0:1, 0:1])
                s.activation(vaB, vtmp, AF.Exp, scale=-0.5).then_inc(ss.h, 1)
            ss.n += 1

            @block.vector
            def _(v, ssn=ss.n):
                v.wait_ge(ss.h, ssn)
                ins = v.scalar_tensor_tensor(vbB, vmu, -1.0, vaB, ALU.mult, ALU.mult)
                if fold_residual:
                    ins = v.tensor_scalar_add(vaB1, vaB, 1.0)
                ins.then_inc(sv.h, 1)
            sv.n += 1
            coeffs_done = sv.n

            if not write_nb:
                return

            ap_done = {}
            for tch in range(NTC):
                @block.tensor
                def _(t, tch=tch):
                    t.wait_ge(sv.h, coeffs_done)
                    if tch >= 1:
                        t.wait_ge(sv.h, ap_done[tch - 1])
                    o = tch * 512
                    t.matmul(psf[6][:], c1x[:], vaB[0:1, o:o + 512],
                             start=True, stop=True)
                    ins = t.matmul(psf[7][:], c1x[:], vbB[0:1, o:o + 512],
                                   start=True, stop=True)
                    if fold_residual:
                        ins = t.matmul(psf[4][:], c1x[:], vaB1[0:1, o:o + 512],
                                       start=True, stop=True)
                    ins.then_inc(st_.h, 1)
                st_.n += 1

                @block.vector
                def _(v, tch=tch, stn=st_.n):
                    v.wait_ge(st_.h, stn)
                    o = tch * 512
                    for dt in range(NDT):
                        v.tensor_mul(F3[:, 0:512], src_view[:, dt, o:o + 512], psf[6][:])
                        if affine is None:
                            ins = v.tensor_add(dst_view[:, dt, o:o + 512],
                                               F3[:, 0:512], psf[7][:])
                        else:
                            v.tensor_add(F3[:, 512:1024], F3[:, 0:512], psf[7][:])
                            g_sb, b_sb = affine
                            ins = v.tensor_scalar(dst_view[:, dt, o:o + 512],
                                                  F3[:, 512:1024], g_sb[:, dt:dt + 1],
                                                  b_sb[:, dt:dt + 1], ALU.mult, ALU.add)
                        if fold_residual:
                            v.tensor_mul(F3[:, 512:1024],
                                         src_view[:, dt, o:o + 512], psf[4][:])
                            ins = v.tensor_add(src_view[:, dt, o:o + 512],
                                               F3[:, 512:1024], psf[7][:])
                        ins.then_inc(sv.h, 1)
                sv.n += NDT
                ap_done[tch] = sv.n

        def norm_slots():
            return dict(mu=vec[0:1, 0:2048], var=vec[0:1, 2048:4096],
                        tmp=vec2[0:1, :], aB=vec2[0:1, :],
                        bB=vec[0:1, 2048:4096], aB1=vec[0:1, 0:2048])

        def phase_attn_a():
            emit_norm(X["v"], norm_slots(), write_nb=True, fold_residual=True)
            barrier()

            @block.gpsimd
            def _(g):
                g.dma_start(out=B4[:, 4096:6144],
                            in_=inp["maskdiag"].ap()[:, 0:2048]).then_inc(gq.h, 16)
                g.dma_start(out=B3[:, 6144:8192],
                            in_=inp["maskdiag"].ap()[:, 2048:4096]).then_inc(gq.h, 16)
            gq.n += 32
            masks_ld = gq.n

            q2 = B2[:, 0:2048]
            k2 = B2[:, 2048:4096]
            vd2 = B2[:, 4096:6144]
            vds = B2[:, 6144:8192]
            wq_b = B3[:, 0:2048]
            wk_b = B4[:, 0:2048]
            wvd_b = B5[:, 0:2048]
            wvu_sb = B3[:, 4096:5120]
            attn16 = sm1[:, 16:32]
            csums = sm1[:, 32:40]
            mchunk = F3[:, 0:512]
            junk = F3[:, 512:1024]
            flatv = vec[0:1, 0:2048]

            pair_w = {}
            pair_done = {}
            a_tr_done = 0

            for hp in range(8):
                bo = (hp % 2) * 1024

                @block.gpsimd
                def _(g, hp=hp, bo=bo):
                    if hp >= 2:
                        g.wait_ge(st_.h, pair_done[hp - 2])
                    for wsb, wh in ((wq_b, wq_h), (wk_b, wk_h), (wvd_b, wvd_h)):
                        for n01 in range(2):
                            n = 2 * hp + n01
                            for dt in range(NDT):
                                o = bo + dt * 128 + n01 * 64
                                g.dma_start(out=wsb[:, o:o + 64],
                                            in_=wh.ap()[n, dt * 128:(dt + 1) * 128, :]
                                            ).then_inc(gq.h, 16)
                gq.n += 48 * 16

                @block.tensor
                def _(t, hp=hp, bo=bo, gqn=gq.n, svn=sv.n, ssn=ss.n):
                    t.wait_ge(gq.h, gqn)
                    t.wait_ge(sv.h, svn)
                    t.wait_ge(ss.h, ssn)
                    if hp == 0:
                        t.wait_ge(gq.h, masks_ld)
                    for wi, wsb in enumerate((wq_b, wk_b, wvd_b)):
                        for tch in range(NTC):
                            u = wi * NTC + tch
                            pb = psf[u % 6]
                            if u >= 6:
                                t.wait_ge(sv.h, svn + u - 6 + 1)
                            for dt in range(NDT):
                                ins = t.matmul(pb[:],
                                               wsb[:, bo + dt * 128:bo + (dt + 1) * 128],
                                               nbv[:, dt, tch * 512:(tch + 1) * 512],
                                               start=(dt == 0), stop=(dt == NDT - 1))
                            ins.then_inc(st_.h, 1)
                st_.n += 3 * NTC

                @block.vector
                def _(v, base=st_.n - 3 * NTC):
                    for wi, dst in enumerate((q2, k2, vd2)):
                        for tch in range(NTC):
                            v.wait_ge(st_.h, base + wi * NTC + tch + 1)
                            pb = psf[(wi * NTC + tch) % 6]
                            if wi == 1:
                                v.tensor_scalar_mul(dst[:, tch * 512:(tch + 1) * 512],
                                                    pb[:], 0.125).then_inc(sv.h, 1)
                            else:
                                v.tensor_copy(dst[:, tch * 512:(tch + 1) * 512],
                                              pb[:]).then_inc(sv.h, 1)
                sv.n += 3 * NTC
                proj_ev = sv.n

                sc_ev = {0: ss.n, 1: ss.n, 2: ss.n, 3: ss.n}
                for n01 in range(2):
                    plo = n01 * 64
                    for si in range(16):
                        j0, r = si // 4, si % 4
                        nchunks = 4 - j0
                        mm_at = {}
                        for ci in range(nchunks):
                            j = j0 + ci

                            @block.tensor
                            def _(t, si=si, j=j, ci=ci, plo=plo, pe=proj_ev,
                                  scv=sc_ev[ci % 4]):
                                t.wait_ge(sv.h, pe)
                                t.wait_ge(ss.h, scv)
                                t.matmul(psf[ci][:],
                                         k2[plo:plo + 64, si * 128:(si + 1) * 128],
                                         q2[plo:plo + 64, j * 512:(j + 1) * 512],
                                         start=True, stop=True).then_inc(st_.h, 1)
                            st_.n += 1
                            mm_at[j] = st_.n

                        svn_pre = sv.n

                        @block.vector
                        def _(v, r=r, mm0=mm_at[j0], ssn=ss.n, trd=a_tr_done,
                              svn_pre=sv.n):
                            v.wait_ge(ss.h, ssn)
                            v.wait_ge(st_.h, max(mm0, trd))
                            mb = B4[:, 4096 + r * 512:4096 + (r + 1) * 512]
                            dmk = B3[:, 6144 + r * 512:6144 + (r + 1) * 512]
                            v.scalar_tensor_tensor(junk[:, 0:512], psf[0][:], 1.0,
                                                   dmk, ALU.bypass, ALU.mult,
                                                   accum_out=sm1[:, 0:1]).then_inc(sv.h, 1)
                            v.scalar_tensor_tensor(mchunk, psf[0][:], 1.0, mb,
                                                   ALU.bypass, ALU.add)
                            v.wait_ge(sv.h, svn_pre + 1)
                            v.tensor_scalar_mul(sm1[:, 1:2], sm1[:, 0:1],
                                                -1.0).then_inc(sv.h, 1)
                        sv.n += 2
                        mask_done = sv.n

                        @block.scalar
                        def _(s, j0=j0, nchunks=nchunks, md=mask_done,
                              mm_at=dict(mm_at)):
                            s.wait_ge(sv.h, md)
                            ins = s.activation(junk[:, 0:512], mchunk, AF.Exp,
                                               bias=sm1[:, 1:2], accum_out=csums[:, 0:1])
                            for ci in range(1, nchunks):
                                s.wait_ge(st_.h, mm_at[j0 + ci])
                                ins = s.activation(junk[:, 0:512], psf[ci][:], AF.Exp,
                                                   bias=sm1[:, 1:2],
                                                   accum_out=csums[:, ci:ci + 1])
                            ins.then_inc(ss.h, 1)
                        ss.n += 1
                        for ci in range(nchunks):
                            sc_ev[ci] = ss.n

                        @block.vector
                        def _(v, si=si, nchunks=nchunks, ssn=ss.n, svn_pre=sv.n):
                            v.wait_ge(ss.h, ssn)
                            v.tensor_reduce(sm1[:, 2:3], csums[:, 0:nchunks],
                                            AX.X, ALU.add).then_inc(sv.h, 1)
                            v.wait_ge(sv.h, svn_pre + 1)
                            v.reciprocal(attn16[:, si:si + 1],
                                         sm1[:, 2:3]).then_inc(sv.h, 1)
                        sv.n += 2

                    @block.tensor
                    def _(t, svn=sv.n, scv=dict(sc_ev)):
                        t.wait_ge(sv.h, svn)
                        t.wait_ge(ss.h, max(scv.values()))
                        t.transpose(psf[4][0:16, 0:128], attn16,
                                    cI[:]).then_inc(st_.h, 1)
                    st_.n += 1
                    a_tr_done = st_.n

                    @block.vector
                    def _(v, stn=st_.n):
                        v.wait_ge(st_.h, stn)
                        v.tensor_copy(flat16[0:16, 0:128],
                                      psf[4][0:16, 0:128]).then_inc(sv.h, 1)
                    sv.n += 1

                    @block.sync
                    def _(q, svn=sv.n, sqn=sq.n, stn=st_.n):
                        q.wait_ge(sv.h, svn)
                        q.wait_ge(st_.h, stn)
                        q.dma_start(out=flat_hb.ap()[0:1, 0:2048]
                                    .rearrange("o (s p) -> (o s) p", p=128),
                                    in_=flat16[0:16, 0:128]).then_inc(sq.h, 16)
                        q.wait_ge(sq.h, sqn + 16)
                        q.dma_start(out=flatv,
                                    in_=flat_hb.ap()[0:1, 0:2048]).then_inc(sq.h, 16)
                    sq.n += 32

                    vds_base = sv.n
                    for tch in range(NTC):
                        @block.tensor
                        def _(t, tch=tch, sqn=sq.n, vb=vds_base):
                            t.wait_ge(sq.h, sqn)
                            t.wait_ge(sv.h, vb if tch == 0 else vb + tch)
                            t.matmul(psf[5][:], c1x16[:],
                                     flatv[0:1, tch * 512:(tch + 1) * 512],
                                     start=True, stop=True).then_inc(st_.h, 1)
                        st_.n += 1

                        @block.vector
                        def _(v, tch=tch, plo=plo, stn=st_.n):
                            v.wait_ge(st_.h, stn)
                            v.tensor_mul(vds[plo:plo + 64, tch * 512:(tch + 1) * 512],
                                         vd2[plo:plo + 64, tch * 512:(tch + 1) * 512],
                                         psf[5][plo:plo + 64, :]).then_inc(sv.h, 1)
                        sv.n += 1

                @block.gpsimd
                def _(g, hp=hp):
                    if hp >= 1:
                        g.wait_ge(st_.h, pair_w[hp - 1])
                    for n01 in range(2):
                        n = 2 * hp + n01
                        g.dma_start(out=wvu_sb[n01 * 64:n01 * 64 + 64, :],
                                    in_=wvu_h.ap()[n, :, :]).then_inc(gq.h, 16)
                gq.n += 32

                add_base = sv.n

                @block.tensor
                def _(t, gqn=gq.n, svn=sv.n, ab=add_base):
                    t.wait_ge(gq.h, gqn)
                    t.wait_ge(sv.h, svn)
                    k = 0
                    for ec in range(NDT):
                        for tch in range(NTC):
                            b = 6 + (k % 2)
                            t.wait_ge(sv.h, ab if k < 2 else ab + k - 1)
                            t.matmul(psf[b][:], wvu_sb[:, ec * 128:(ec + 1) * 128],
                                     vds[:, tch * 512:(tch + 1) * 512],
                                     start=True, stop=True).then_inc(st_.h, 1)
                            k += 1
                st_.n += NDT * NTC
                pair_w[hp] = st_.n
                pair_done[hp] = st_.n

                @block.vector
                def _(v, base=st_.n - NDT * NTC):
                    k = 0
                    for ec in range(NDT):
                        for tch in range(NTC):
                            b = 6 + (k % 2)
                            v.wait_ge(st_.h, base + k + 1)
                            v.tensor_add(X["v"][:, ec, tch * 512:(tch + 1) * 512],
                                         X["v"][:, ec, tch * 512:(tch + 1) * 512],
                                         psf[b][:]).then_inc(sv.h, 1)
                            k += 1
                sv.n += NDT * NTC

            barrier()

        def act_view(u):
            if u < 16:
                return B1[:, u * 1024:(u + 1) * 1024]
            if u < 24:
                return B3[:, (u - 16) * 1024:(u - 15) * 1024]
            return B4[:, (u - 24) * 1024:(u - 23) * 1024]

        def phase_moe():
            nbm = {}
            for dt in range(NDT):
                pool = B2 if dt < 4 else B5
                nbm[dt] = pool[:, (dt % 4) * 2048:((dt % 4) + 1) * 2048]

            class Shim:
                def __getitem__(self, key):
                    _, dt, tsl = key
                    return nbm[dt][:, tsl]
            nbw = Shim()
            emit_norm(X["v"], norm_slots(), write_nb=True, dst_view=nbw)
            barrier()

            gw = W5[:, 4096:4160]
            gsl = F3[:, 0:1024]
            crow = vec[0:1, 0:1024]

            @block.gpsimd
            def _(g):
                g.dma_start(out=gw.rearrange("p (dt e) -> p dt e", dt=NDT),
                            in_=inp["moe_gwr"].ap()
                            .rearrange("dt p e -> p dt e")).then_inc(gq.h, 16)
            gq.n += 16
            gw_ld = gq.n

            w1xa = inp["moe_w1x"].ap()
            w1ga = inp["moe_w1g"].ap()
            w2ta = inp["moe_w2t"].ap()

            for th in range(2):
                tho = th * 1024
                for tc in range(2):
                    @block.tensor
                    def _(t, tc=tc, gqn=gw_ld, svn=sv.n):
                        t.wait_ge(gq.h, gqn)
                        t.wait_ge(sv.h, svn)
                        for dt in range(NDT):
                            ins = t.matmul(psf[0][0:8, :], gw[:, dt * 8:(dt + 1) * 8],
                                           nbw[:, dt, tho + tc * 512:tho + (tc + 1) * 512],
                                           start=(dt == 0), stop=(dt == NDT - 1))
                        ins.then_inc(st_.h, 1)
                    st_.n += 1

                    @block.vector
                    def _(v, tc=tc, stn=st_.n):
                        v.wait_ge(st_.h, stn)
                        v.tensor_scalar(F3[0:8, tc * 512:(tc + 1) * 512],
                                        psf[0][0:8, :], gbc[:, 0:1], None,
                                        ALU.add).then_inc(sv.h, 1)
                    sv.n += 1
                logits_done = sv.n

                for tt in range(8):
                    @block.tensor
                    def _(t, tt=tt, ld=logits_done, svn=sv.n):
                        t.wait_ge(sv.h, max(ld, svn))
                        t.transpose(psf[1][:, 0:8],
                                    F3[0:8, tt * 128:(tt + 1) * 128],
                                    cI[0:8, 0:8]).then_inc(st_.h, 1)
                    st_.n += 1

                    def vstep(fn):
                        @block.vector
                        def _(v, svn=sv.n):
                            v.wait_ge(sv.h, svn)
                            fn(v).then_inc(sv.h, 1)
                        sv.n += 1

                    def sbounce():
                        @block.scalar
                        def _(s, svn=sv.n):
                            s.wait_ge(sv.h, svn)
                            s.sem_inc(ss.h, 1)
                        ss.n += 1

                        @block.vector
                        def _(v, ssn=ss.n):
                            v.wait_ge(ss.h, ssn)

                    @block.vector
                    def _(v, stn=st_.n):
                        v.wait_ge(st_.h, stn)
                        v.tensor_copy(sm1[:, 0:8], psf[1][:, 0:8]).then_inc(sv.h, 1)
                    sv.n += 1
                    vstep(lambda v: v.tensor_reduce(sm1[:, 8:9], sm1[:, 0:8],
                                                    AX.X, ALU.max))
                    sbounce()
                    vstep(lambda v: v.tensor_scalar(sm1[:, 9:17], sm1[:, 0:8],
                                                    sm1[:, 8:9], None, ALU.is_equal))
                    vstep(lambda v: v.scalar_tensor_tensor(
                        sm1[:, 17:25], sm1[:, 9:17], -1e30, sm1[:, 0:8],
                        ALU.mult, ALU.add))
                    vstep(lambda v: v.tensor_reduce(sm1[:, 25:26], sm1[:, 17:25],
                                                    AX.X, ALU.max))
                    vstep(lambda v: v.tensor_sub(sm1[:, 34:35], sm1[:, 8:9],
                                                 sm1[:, 25:26]))

                    @block.scalar
                    def _(s, svn=sv.n):
                        s.wait_ge(sv.h, svn)
                        s.activation(sm1[:, 35:36], sm1[:, 34:35],
                                     AF.Sigmoid).then_inc(ss.h, 1)
                    ss.n += 1

                    @block.vector
                    def _(v, ssn=ss.n):
                        v.wait_ge(ss.h, ssn)
                        v.tensor_scalar(sm1[:, 26:34], sm1[:, 17:25], sm1[:, 25:26],
                                        None, ALU.is_equal).then_inc(sv.h, 1)
                    sv.n += 1
                    vstep(lambda v: v.tensor_scalar(sm1[:, 36:37], sm1[:, 35:36],
                                                    -1.0, 1.0, ALU.mult, ALU.add))
                    sbounce()
                    vstep(lambda v: v.tensor_scalar(sm1[:, 0:8], sm1[:, 26:34],
                                                    sm1[:, 36:37], None, ALU.mult))
                    vstep(lambda v: v.scalar_tensor_tensor(
                        sm1[:, 17:25], sm1[:, 9:17], sm1[:, 35:36], sm1[:, 0:8],
                        ALU.mult, ALU.add))

                    @block.tensor
                    def _(t, svn=sv.n):
                        t.wait_ge(sv.h, svn)
                        t.transpose(psf[2][0:8, 0:128], sm1[:, 17:25],
                                    cI[:]).then_inc(st_.h, 1)
                    st_.n += 1

                    @block.vector
                    def _(v, tt=tt, stn=st_.n):
                        v.wait_ge(st_.h, stn)
                        v.tensor_copy(F1[0:8, tt * 128:(tt + 1) * 128],
                                      psf[2][0:8, 0:128]).then_inc(sv.h, 1)
                    sv.n += 1

                @block.sync
                def _(q, svn=sv.n):
                    q.wait_ge(sv.h, svn)
                    q.dma_start(out=coef_hb.ap(),
                                in_=F1[0:8, 0:1024]).then_inc(sq.h, 16)
                sq.n += 16
                coef_ld = sq.n
                barrier()

                if "MDBG" in stages and th == 0:
                    @block.sync
                    def _(q):
                        q.dma_start(out=out_h.ap()[0:8, :],
                                    in_=F3[0:8, 0:1024]).then_inc(sq.h, 16)
                        q.dma_start(out=out_h.ap()[16:24, :],
                                    in_=F1[0:8, 0:1024]).then_inc(sq.h, 16)
                    sq.n += 32

                elist = range(NE)
                if "MDBG" in stages:
                    elist = [0] if th == 0 else []
                for e in elist:
                    prev_st = st_.n
                    @block.sync
                    def _(q, e=e, chb=coef_ld, stn=st_.n):
                        q.wait_ge(sq.h, chb)
                        if e > 0:
                            q.wait_ge(st_.h, stn)
                        q.dma_start(out=crow,
                                    in_=coef_hb.ap()[e:e + 1, :]).then_inc(sq.h, 16)
                    sq.n += 16

                    @block.gpsimd
                    def _(g, e=e, svn=sv.n, ssn=ss.n):
                        if e > 0:
                            g.wait_ge(sv.h, svn)
                            g.wait_ge(ss.h, ssn)
                        g.dma_start(out=b1sb[:],
                                    in_=inp["moe_b1c"].ap()[e]).then_inc(gq.h, 16)
                        g.dma_start(out=b2sb[:],
                                    in_=inp["moe_b2c"].ap()[e]).then_inc(gq.h, 16)
                    gq.n += 32
                    bias_ld = gq.n

                    for tc in range(2):
                        @block.tensor
                        def _(t, tc=tc, sqn=sq.n, svn=sv.n):
                            t.wait_ge(sq.h, sqn)
                            t.wait_ge(sv.h, svn)
                            t.matmul(psf[7][:], c1x[:],
                                     crow[0:1, tc * 512:(tc + 1) * 512],
                                     start=True, stop=True).then_inc(st_.h, 1)
                        st_.n += 1

                        @block.vector
                        def _(v, tc=tc, stn=st_.n):
                            v.wait_ge(st_.h, stn)
                            v.tensor_copy(gsl[:, tc * 512:(tc + 1) * 512],
                                          psf[7][:]).then_inc(sv.h, 1)
                        sv.n += 1
                    g_ready = sv.n

                    if "MDBG" in stages and th == 0 and e == 0:
                        @block.sync
                        def _(q, gr=g_ready):
                            q.wait_ge(sv.h, gr)
                            q.dma_start(out=out_h.ap()[32:33, :],
                                        in_=crow).then_inc(sq.h, 16)
                            q.dma_start(out=out_h.ap()[48:176, :],
                                        in_=gsl[:, 0:1024]).then_inc(sq.h, 16)
                        sq.n += 32

                    load_mk = {}
                    evac_sv = {}
                    evac_ss = {}
                    for u in range(32):
                        slot = (u % 2)
                        wx = W5[:, slot * 2048:slot * 2048 + 1024]
                        wg = W5[:, slot * 2048 + 1024:slot * 2048 + 2048]

                        @block.gpsimd
                        def _(g, u=u, e=e, wx=wx, wg=wg, pst=prev_st):
                            if u >= 2:
                                g.wait_ge(st_.h, load_mk[u - 2])
                            else:
                                g.wait_ge(st_.h, pst)
                            g.dma_start(
                                out=wx.rearrange("p (dt f) -> p dt f", dt=NDT),
                                in_=w1xa[e, u]).then_inc(gq.h, 16)
                            g.dma_start(
                                out=wg.rearrange("p (dt f) -> p dt f", dt=NDT),
                                in_=w1ga[e, u]).then_inc(gq.h, 16)
                        gq.n += 32

                        if ("MDBG" in stages and th == 0 and e == 0
                                and u == 31):
                            @block.vector
                            def _(v, svn=sv.n):
                                v.wait_ge(sv.h, svn)
                                v.tensor_copy(act_view(28)[:, 0:512], psf[4][:])
                                v.tensor_copy(act_view(28)[:, 512:1024],
                                              psf[5][:]).then_inc(sv.h, 1)
                            sv.n += 1

                        bb = (u % 2) * 4
                        gr_w = g_ready
                        if "MDBG" in stages and th == 0 and e == 0 and u == 31:
                            gr_w = sv.n
                        @block.tensor
                        def _(t, u=u, wx=wx, wg=wg, bb=bb, gqn=gq.n, gr=gr_w):
                            t.wait_ge(gq.h, gqn)
                            t.wait_ge(sv.h, gr)
                            if u >= 2:
                                t.wait_ge(sv.h, evac_sv[u - 2])
                                t.wait_ge(ss.h, evac_ss[u - 2])
                            for tc in range(2):
                                for dt in range(NDT):
                                    t.matmul(psf[bb + tc][:],
                                             wx[:, dt * 128:(dt + 1) * 128],
                                             nbw[:, dt, tho + tc * 512:tho + (tc + 1) * 512],
                                             start=(dt == 0), stop=(dt == NDT - 1))
                                for dt in range(NDT):
                                    ins = t.matmul(psf[bb + 2 + tc][:],
                                                   wg[:, dt * 128:(dt + 1) * 128],
                                                   nbw[:, dt, tho + tc * 512:tho + (tc + 1) * 512],
                                                   start=(dt == 0), stop=(dt == NDT - 1))
                            ins.then_inc(st_.h, 1)
                        st_.n += 1
                        load_mk[u] = st_.n

                        @block.scalar
                        def _(s, u=u, bb=bb, stn=st_.n, bld=bias_ld, psv=evac_sv):
                            s.wait_ge(st_.h, stn)
                            s.wait_ge(gq.h, bld)
                            if u >= 1:
                                s.wait_ge(sv.h, psv[u - 1])
                            for tc in range(2):
                                ins = s.activation(
                                    W5[:, 4160 + tc * 512:4160 + (tc + 1) * 512],
                                    psf[bb + 2 + tc][:], AF.Silu,
                                    bias=b1sb[:, 32 + u:33 + u])
                            ins.then_inc(ss.h, 1)
                        ss.n += 1
                        evac_ss[u] = ss.n

                        @block.vector
                        def _(v, u=u, bb=bb, ssn=ss.n, bld=bias_ld):
                            v.wait_ge(ss.h, ssn)
                            v.wait_ge(gq.h, bld)
                            for tc in range(2):
                                v.scalar_tensor_tensor(
                                    act_view(u)[:, tc * 512:(tc + 1) * 512],
                                    psf[bb + tc][:], b1sb[:, u:u + 1],
                                    W5[:, 4160 + tc * 512:4160 + (tc + 1) * 512],
                                    ALU.add, ALU.mult).then_inc(sv.h, 1)
                        sv.n += 2
                        evac_sv[u] = sv.n
                    h_done_sv = sv.n
                    h_done_st = st_.n

                    if "MDBG" in stages and th == 0 and e == 0:
                        @block.gpsimd
                        def _(g, hd=h_done_sv):
                            g.wait_ge(sv.h, hd)
                            g.dma_start(out=out_h.ap()[200:328, :],
                                        in_=act_view(0)).then_inc(gq.h, 16)
                            g.dma_start(out=out_h.ap()[340:468, :],
                                        in_=act_view(31)).then_inc(gq.h, 16)
                            g.dma_start(out=out_h.ap()[480:608, :],
                                        in_=W5[:, 2048:3072]).then_inc(gq.h, 16)
                            g.dma_start(out=out_h.ap()[620:748, :],
                                        in_=W5[:, 3072:4096]).then_inc(gq.h, 16)
                            g.dma_start(out=out_h.ap()[760:888, :],
                                        in_=W5[:, 4160:5184]).then_inc(gq.h, 16)
                            g.dma_start(out=out_h.ap()[900:1028, :],
                                        in_=act_view(28)).then_inc(gq.h, 16)
                            g.dma_start(out=out_h.ap()[1040:1168, 0:64],
                                        in_=b1sb[:]).then_inc(gq.h, 16)
                        gq.n += 112
                        continue

                    for tc in range(2):
                        prev_vst = st_.n
                        vload = {}
                        for v_ in range(16):
                            slot = v_ % 2
                            wsl = W5[:, slot * 2048:(slot + 1) * 2048]

                            @block.gpsimd
                            def _(g, v_=v_, tc=tc, e=e, wsl=wsl, pvs=prev_vst):
                                if v_ >= 2:
                                    g.wait_ge(st_.h, vload[v_ - 2])
                                else:
                                    g.wait_ge(st_.h, pvs)
                                g.dma_start(
                                    out=wsl.rearrange("p (ft d) -> p ft d", ft=2),
                                    in_=w2ta[e, v_]).then_inc(gq.h, 16)
                            gq.n += 16

                            @block.tensor
                            def _(t, v_=v_, tc=tc, wsl=wsl, gqn=gq.n,
                                  hsv=h_done_sv, svn=sv.n):
                                t.wait_ge(gq.h, gqn)
                                if v_ == 0:
                                    t.wait_ge(sv.h, hsv if tc == 0 else svn)
                                for dt in range(NDT):
                                    for ft in range(2):
                                        ins = t.matmul(
                                            psf[dt][:],
                                            wsl[:, ft * 1024 + dt * 128:ft * 1024 + (dt + 1) * 128],
                                            act_view(2 * v_ + ft)[:, tc * 512:(tc + 1) * 512],
                                            start=(v_ == 0 and ft == 0),
                                            stop=(v_ == 15 and ft == 1))
                                ins.then_inc(st_.h, 1)
                            st_.n += 1
                            vload[v_] = st_.n

                        @block.vector
                        def _(v, tc=tc, th=th, stn=st_.n, bld=bias_ld, svn=sv.n):
                            v.wait_ge(st_.h, stn)
                            v.wait_ge(gq.h, bld)
                            for dt in range(NDT):
                                v.scalar_tensor_tensor(
                                    W5[:, 4160:4672],
                                    psf[dt][:], b2sb[:, dt:dt + 1],
                                    gsl[:, tc * 512:(tc + 1) * 512],
                                    ALU.add, ALU.mult).then_inc(sv.h, 1)
                                v.wait_ge(sv.h, svn + 2 * dt + 1)
                                v.tensor_add(
                                    X["v"][:, dt, tho + tc * 512:tho + (tc + 1) * 512],
                                    X["v"][:, dt, tho + tc * 512:tho + (tc + 1) * 512],
                                    W5[:, 4160:4672]).then_inc(sv.h, 1)
                        sv.n += 2 * NDT
                barrier()
            barrier()

        def phase_attn_b():
            emit_norm(X["v"], norm_slots(), write_nb=False)
            aB1 = vec2[0:1, :]
            bB1 = vec[0:1, 2048:4096]
            barrier()

            @block.sync
            def _(q):
                q.dma_start(out=ngc[:], in_=inp["b_ng"].ap()).then_inc(sq.h, 16)
                q.dma_start(out=nbc2[:], in_=inp["b_nb"].ap()).then_inc(sq.h, 16)
                q.dma_start(out=fgc[:], in_=inp["b_fg"].ap()).then_inc(sq.h, 16)
                q.dma_start(out=fbc[:], in_=inp["b_fb"].ap()).then_inc(sq.h, 16)
                q.dma_start(out=f1bsb[:], in_=inp["b_ff1b"].ap()).then_inc(sq.h, 16)
                q.dma_start(out=f2bsb[:], in_=inp["b_ff2b"].ap()).then_inc(sq.h, 16)
            sq.n += 96
            affld = sq.n

            for tch in range(NTC):
                o = tch * 512

                @block.tensor
                def _(t, o=o, svn=sv.n):
                    t.wait_ge(sv.h, svn)
                    t.matmul(psf[6][:], c1x[:], aB1[0:1, o:o + 512],
                             start=True, stop=True)
                    t.matmul(psf[7][:], c1x[:], bB1[0:1, o:o + 512],
                             start=True, stop=True).then_inc(st_.h, 1)
                st_.n += 1
                bc_done = st_.n

                for dt in range(NDT):
                    @block.vector
                    def _(v, dt=dt, o=o, bd=bc_done, svn=sv.n, ssn=ss.n):
                        v.wait_ge(st_.h, bd)
                        v.wait_ge(ss.h, ssn)
                        v.tensor_mul(F1[:, 0:512], X["v"][:, dt, o:o + 512],
                                     psf[6][:]).then_inc(sv.h, 1)
                        v.wait_ge(sv.h, svn + 1)
                        v.tensor_add(F1[:, 0:512], F1[:, 0:512],
                                     psf[7][:]).then_inc(sv.h, 1)
                    sv.n += 2

                    @block.scalar
                    def _(s, svn=sv.n):
                        s.wait_ge(sv.h, svn)
                        s.activation(F1[:, 512:1024], F1[:, 0:512],
                                     AF.Square).then_inc(ss.h, 1)
                    ss.n += 1

                    @block.tensor
                    def _(t, dt=dt, ssn=ss.n):
                        t.wait_ge(ss.h, ssn)
                        t.matmul(psf[4][0:1, :], cCol[:], F1[:, 0:512],
                                 start=(dt == 0), stop=(dt == NDT - 1))
                        t.matmul(psf[5][0:1, :], cCol[:], F1[:, 512:1024],
                                 start=(dt == 0), stop=(dt == NDT - 1)
                                 ).then_inc(st_.h, 1)
                    st_.n += 1

                mu2 = F1[0:1, 0:512]
                r2 = F1[0:1, 512:1024]

                @block.vector
                def _(v, stn=st_.n, svn=sv.n):
                    v.wait_ge(st_.h, stn)
                    v.tensor_scalar_mul(mu2, psf[4][0:1, :],
                                        1.0 / D).then_inc(sv.h, 1)
                    v.wait_ge(sv.h, svn + 1)
                    v.tensor_mul(r2, mu2, mu2).then_inc(sv.h, 1)
                    v.wait_ge(sv.h, svn + 2)
                    v.scalar_tensor_tensor(r2, psf[5][0:1, :], 1.0 / D, r2,
                                           ALU.mult, ALU.subtract).then_inc(sv.h, 1)
                sv.n += 3

                @block.scalar
                def _(s, svn=sv.n):
                    s.wait_ge(sv.h, svn)
                    s.activation(r2, r2, AF.Ln, bias=cVal[0:1, 0:1])
                    s.activation(r2, r2, AF.Exp, scale=-0.5).then_inc(ss.h, 1)
                ss.n += 1

                @block.vector
                def _(v, o=o, ssn=ss.n, svn=sv.n):
                    v.wait_ge(ss.h, ssn)
                    v.tensor_sub(bB1[0:1, o:o + 512], bB1[0:1, o:o + 512],
                                 mu2).then_inc(sv.h, 1)
                    v.wait_ge(sv.h, svn + 1)
                    v.tensor_mul(bB1[0:1, o:o + 512], bB1[0:1, o:o + 512],
                                 r2).then_inc(sv.h, 1)
                    v.wait_ge(sv.h, svn + 2)
                    v.tensor_mul(aB1[0:1, o:o + 512], aB1[0:1, o:o + 512],
                                 r2).then_inc(sv.h, 1)
                sv.n += 3
            barrier()
            rho = aB1
            sig = bB1

            hh3 = B1[:].rearrange("p (dt t) -> p dt t", dt=NDT)
            for tch in range(NTC):
                o = tch * 512

                @block.tensor
                def _(t, o=o, svn=sv.n):
                    t.wait_ge(sv.h, svn)
                    t.matmul(psf[6][:], c1x[:], rho[0:1, o:o + 512],
                             start=True, stop=True)
                    t.matmul(psf[7][:], c1x[:], sig[0:1, o:o + 512],
                             start=True, stop=True).then_inc(st_.h, 1)
                st_.n += 1

                for dt in range(NDT):
                    @block.vector
                    def _(v, dt=dt, o=o, stn=st_.n, svn=sv.n, affq=affld):
                        v.wait_ge(st_.h, stn)
                        v.wait_ge(sq.h, affq)
                        v.tensor_mul(F1[:, 0:512], X["v"][:, dt, o:o + 512],
                                     psf[6][:])
                        v.tensor_scalar(F3[:, 0:512], psf[7][:],
                                        ngc[:, dt:dt + 1], nbc2[:, dt:dt + 1],
                                        ALU.mult, ALU.add).then_inc(sv.h, 1)
                        v.wait_ge(sv.h, svn + 1)
                        v.scalar_tensor_tensor(hh3[:, dt, o:o + 512],
                                               F1[:, 0:512], ngc[:, dt:dt + 1],
                                               F3[:, 0:512], ALU.mult,
                                               ALU.add).then_inc(sv.h, 1)
                    sv.n += 2
            barrier()

            for half in range(2):
                @block.sync
                def _(q, half=half, svn=sv.n):
                    q.wait_ge(sv.h, svn)
                    q.dma_start(out=xspill.ap()[:, half * 8192:(half + 1) * 8192],
                                in_=X["sb"][:, half * 8192:(half + 1) * 8192]
                                ).then_inc(sq.h, 16)
                sq.n += 16
            barrier()
            es_x.close()

            es_b = contextlib.ExitStack()

            def sb_b(name, shape, dtype):
                return es_b.enter_context(nc.sbuf_tensor(name, shape, dtype))

            xs = sb_b("xs", [128, 16 * 1028], BF16)
            xs3 = xs[:].rearrange("p (et t) -> p et t", et=16)
            dtv = sb_b("dtv", [128, 8 * 512], F32)
            dtv3 = dtv[:].rearrange("p (eh t) -> p eh t", eh=8)
            xpj = sb_b("xpj", [128, 16 * 96], BF16)
            dtT = sb_b("dtT", [64, 2048], BF16)
            Asb = sb_b("Asb", [128, 256], F32)
            state = sb_b("state", [128, 256], F32)
            cwsb = sb_b("cwsb", [128, 64], F32)
            cbsb = sb_b("cbsb", [128, 16], F32)
            dtbsb = sb_b("dtbsb", [128, 16], F32)
            Dsb = sb_b("Dsb", [128, 16], F32)
            tailsv = sb_b("tailsv", [128, 48], BF16)
            Brow2 = [sb_b("Brow0", [1, 512], BF16), sb_b("Brow1", [1, 512], BF16)]
            Crow2 = [sb_b("Crow0", [1, 512], BF16), sb_b("Crow1", [1, 512], BF16)]

            wpc = [W5[:, 0:2048], W5[:, 2048:4096]]
            hsl = [W5[:, 0:512], W5[:, 512:1024]]
            Bbb2 = [W5[:, 1024:1536], W5[:, 4096:4608]]
            Cbb2 = [W5[:, 1536:2048], W5[:, 4608:5120]]
            Bbb = Bbb2[0]
            Cbb = Cbb2[0]
            dbx = W5[:, 2048:2560]
            psl = W5[:, 3072:3584]
            dtl_sb = W5[0:64, 2560:3072]
            Bsb = W5[0:16, 3584:4096]
            Csb = W5[32:48, 3584:4096]

            def fwdsl(ot, lo, hi):
                pool = B4 if ot < 4 else B5
                base = (ot % 4) * 2048
                return pool[:, base + lo:base + hi]

            scan_prev = [0, 0]
            da_last = {0: 0, 1: 0}
            bc_ev = {}

            def mamba(m, is_rev):
                @block.gpsimd
                def _(g, m=m, svn=sv.n, ssn=ss.n, stn=st_.n):
                    g.wait_ge(sv.h, svn)
                    g.wait_ge(ss.h, ssn)
                    g.wait_ge(st_.h, stn)
                    g.dma_start(out=xpj[:].rearrange("p (et c) -> p et c", et=16),
                                in_=inp[f"b_{m}_xprojT"].ap()
                                .rearrange("(et p) c -> p et c", p=128)
                                ).then_inc(gq.h, 16)
                    g.dma_start(out=dtT[:],
                                in_=inp[f"b_{m}_dtT"].ap()).then_inc(gq.h, 16)
                    g.dma_start(out=Asb[:].rearrange("p (et s) -> p et s", et=16),
                                in_=inp[f"b_{m}_Alog"].ap()
                                .rearrange("et p s -> p et s")).then_inc(gq.h, 16)
                    g.dma_start(out=cwsb[:].rearrange("p (et k) -> p et k", et=16),
                                in_=inp[f"b_{m}_cw"].ap()
                                .rearrange("et p k -> p et k")).then_inc(gq.h, 16)
                    g.dma_start(out=cbsb[:], in_=inp[f"b_{m}_cb"].ap()).then_inc(gq.h, 16)
                    g.dma_start(out=dtbsb[:], in_=inp[f"b_{m}_dtb"].ap()).then_inc(gq.h, 16)
                    g.dma_start(out=Dsb[:], in_=inp[f"b_{m}_D"].ap()).then_inc(gq.h, 16)
                gq.n += 112
                smld = gq.n

                @block.scalar
                def _(s, gqn=smld):
                    s.wait_ge(gq.h, gqn)
                    s.activation(Asb[:], Asb[:], AF.Exp).then_inc(ss.h, 1)
                ss.n += 1

                @block.vector
                def _(v, ssn=ss.n):
                    v.wait_ge(ss.h, ssn)
                    v.tensor_scalar_mul(Asb[:], Asb[:], -1.0).then_inc(sv.h, 1)
                sv.n += 1

                @block.gpsimd
                def _(g, svn=sv.n):
                    g.wait_ge(sv.h, svn)
                    g.memset(state[:], 0.0).then_inc(gq.h, 1)
                gq.n += 1

                inw = inp[f"b_{m}_inwt"].ap()
                outw = inp[f"b_{m}_outT"].ap().rearrange("(ct p) o -> ct p o", p=128)

                for c in range(2):
                    j0 = c * 1024

                    @block.vector
                    def _(v, c=c, svn=sv.n, stn=st_.n, gqn=gq.n):
                        v.wait_ge(sv.h, svn)
                        v.wait_ge(st_.h, stn)
                        v.wait_ge(gq.h, gqn)
                        ins = None
                        for et in range(16):
                            if c == 0:
                                ins = v.memset(xs3[:, et, 0:4], 0.0)
                            else:
                                ins = v.tensor_copy(xs3[:, et, 1:4],
                                                    tailsv[:, et * 3:et * 3 + 3])
                        ins.then_inc(sv.h, 1)
                    sv.n += 1
                    hdr_done = sv.n

                    ip_mm = {}
                    ip_ev = {}
                    for ep in range(8):
                        slot = wpc[ep % 2]

                        @block.gpsimd
                        def _(g, ep=ep, slot=slot, stn=st_.n):
                            if ep >= 2:
                                g.wait_ge(st_.h, ip_mm[ep - 2])
                            else:
                                g.wait_ge(st_.h, stn)
                            g.dma_start(
                                out=slot.rearrange("p (dt e) -> p dt e", dt=NDT),
                                in_=inw[ep]).then_inc(gq.h, 16)
                        gq.n += 16

                        @block.tensor
                        def _(t, ep=ep, slot=slot, gqn=gq.n, hd=hdr_done):
                            t.wait_ge(gq.h, gqn)
                            t.wait_ge(sv.h, hd)
                            if ep >= 2:
                                t.wait_ge(sv.h, ip_ev[ep - 2])
                            bb = 4 * (ep % 2)
                            for e2 in range(2):
                                for h2 in range(2):
                                    src = (j0 + h2 * 512 if not is_rev
                                           else 2048 - j0 - (h2 + 1) * 512)
                                    for dt in range(NDT):
                                        ins = t.matmul(
                                            psf[bb + 2 * e2 + h2][:],
                                            slot[:, dt * 256 + e2 * 128:
                                                 dt * 256 + e2 * 128 + 128],
                                            hh3[:, dt, src:src + 512],
                                            start=(dt == 0), stop=(dt == NDT - 1))
                            ins.then_inc(st_.h, 1)
                        st_.n += 1
                        ip_mm[ep] = st_.n

                        @block.vector
                        def _(v, ep=ep, stn=st_.n):
                            v.wait_ge(st_.h, stn)
                            bb = 4 * (ep % 2)
                            ins = None
                            for e2 in range(2):
                                et = 2 * ep + e2
                                for h2 in range(2):
                                    dst = xs3[:, et,
                                              4 + h2 * 512:4 + (h2 + 1) * 512]
                                    if is_rev:
                                        dst = dst[:, ::-1]
                                    ins = v.tensor_copy(dst,
                                                        psf[bb + 2 * e2 + h2][:])
                            ins.then_inc(sv.h, 1)
                        sv.n += 1
                        ip_ev[ep] = sv.n
                    inproj_done = sv.n

                    @block.gpsimd
                    def _(g, ipd=inproj_done):
                        g.wait_ge(sv.h, ipd)
                        ins = None
                        for et in range(16):
                            ins = g.tensor_copy(tailsv[:, et * 3:et * 3 + 3],
                                                xs3[:, et, 1025:1028])
                        ins.then_inc(gq.h, 1)
                    gq.n += 1
                    tail_done = gq.n

                    for et in range(16):
                        @block.vector
                        def _(v, et=et, td=tail_done, svn=sv.n, ssn=ss.n,
                              smq=smld):
                            v.wait_ge(gq.h, max(td, smq))
                            v.wait_ge(ss.h, ssn)
                            v.tensor_scalar(F1[:, 0:1024], xs3[:, et, 1:1025],
                                            cwsb[:, et * 4:et * 4 + 1], None,
                                            ALU.mult).then_inc(sv.h, 1)
                            for k in range(1, 4):
                                v.wait_ge(sv.h, svn + k)
                                v.scalar_tensor_tensor(
                                    F1[:, 0:1024], xs3[:, et, 1 + k:1025 + k],
                                    cwsb[:, et * 4 + k:et * 4 + k + 1],
                                    F1[:, 0:1024], ALU.mult,
                                    ALU.add).then_inc(sv.h, 1)
                        sv.n += 4

                        @block.scalar
                        def _(s, et=et, svn=sv.n):
                            s.wait_ge(sv.h, svn)
                            s.activation(xs3[:, et, 4:1028], F1[:, 0:1024],
                                         AF.Silu,
                                         bias=cbsb[:, et:et + 1]).then_inc(ss.h, 1)
                        ss.n += 1
                    conv_done = ss.n

                    for sb2 in range(2):
                        so = sb2 * 512

                        @block.tensor
                        def _(t, so=so, cd=conv_done, svn=sv.n, ssn=ss.n):
                            t.wait_ge(ss.h, max(cd, ssn))
                            t.wait_ge(sv.h, svn)
                            for et in range(16):
                                st1 = (et == 0)
                                sp1 = (et == 15)
                                t.matmul(psf[0][0:64, :],
                                         xpj[:, et * 96:et * 96 + 64],
                                         xs3[:, et, 4 + so:4 + so + 512],
                                         start=st1, stop=sp1)
                                t.matmul(psf[1][0:16, :],
                                         xpj[:, et * 96 + 64:et * 96 + 80],
                                         xs3[:, et, 4 + so:4 + so + 512],
                                         start=st1, stop=sp1)
                                ins = t.matmul(psf[2][0:16, :],
                                               xpj[:, et * 96 + 80:et * 96 + 96],
                                               xs3[:, et, 4 + so:4 + so + 512],
                                               start=st1, stop=sp1)
                            ins.then_inc(st_.h, 1)
                        st_.n += 1

                        @block.vector
                        def _(v, stn=st_.n):
                            v.wait_ge(st_.h, stn)
                            v.tensor_copy(dtl_sb[:], psf[0][0:64, :])
                            v.tensor_copy(Bsb[:], psf[1][0:16, :])
                            v.tensor_copy(Csb[:],
                                          psf[2][0:16, :]).then_inc(sv.h, 1)
                        sv.n += 1
                        xp_done = sv.n

                        for half in range(2):
                            ets = list(range(half * 8, half * 8 + 8))
                            dtm_ev = {}
                            for et in ets:
                                eh = et % 8

                                @block.tensor
                                def _(t, et=et, xd=xp_done, svn=sv.n):
                                    t.wait_ge(sv.h, max(xd, svn))
                                    t.matmul(psf[3][:],
                                             dtT[:, et * 128:(et + 1) * 128],
                                             dtl_sb[:], start=True,
                                             stop=True).then_inc(st_.h, 1)
                                st_.n += 1

                                @block.vector
                                def _(v, eh=eh, stn=st_.n, smq=smld):
                                    v.wait_ge(st_.h, stn)
                                    v.wait_ge(gq.h, smq)
                                    v.tensor_scalar(dtv3[:, eh, :], psf[3][:],
                                                    dtbsb[:, et:et + 1], None,
                                                    ALU.add).then_inc(sv.h, 1)
                                sv.n += 1
                                dtm_ev[et] = sv.n

                                @block.scalar
                                def _(s, eh=eh, svn=sv.n, ssn=ss.n):
                                    s.wait_ge(sv.h, svn)
                                    s.wait_ge(ss.h, ssn)
                                    s.activation(F3[:, 0:512], dtv3[:, eh, :],
                                                 AF.Abs)
                                    s.activation(F3[:, 0:512], F3[:, 0:512],
                                                 AF.Exp, scale=-1.0)
                                    s.activation(F3[:, 0:512], F3[:, 0:512],
                                                 AF.Ln, bias=1.0).then_inc(ss.h, 1)
                                ss.n += 1

                                @block.vector
                                def _(v, et=et, eh=eh, so=so, ssn=ss.n, svn=sv.n):
                                    v.wait_ge(ss.h, ssn)
                                    v.tensor_scalar_max(F3[:, 512:1024],
                                                        dtv3[:, eh, :],
                                                        0.0).then_inc(sv.h, 1)
                                    v.wait_ge(sv.h, svn + 1)
                                    v.tensor_add(dtv3[:, eh, :], F3[:, 512:1024],
                                                 F3[:, 0:512]).then_inc(sv.h, 1)
                                    v.wait_ge(sv.h, svn + 2)
                                    v.tensor_scalar(B3[:, et * 512:(et + 1) * 512],
                                                    xs3[:, et, 4 + so:4 + so + 512],
                                                    Dsb[:, et:et + 1], None,
                                                    ALU.mult).then_inc(sv.h, 1)
                                    v.wait_ge(sv.h, svn + 3)
                                    v.tensor_mul(xs3[:, et, 4 + so:4 + so + 512],
                                                 dtv3[:, eh, :],
                                                 xs3[:, et, 4 + so:4 + so + 512]
                                                 ).then_inc(sv.h, 1)
                                sv.n += 4
                            dt_half_done = sv.n

                            sdone = {}
                            row_mm = {}
                            for s16 in range(16):
                                par = s16 % 2
                                Bq, Cq = Bbb2[par], Cbb2[par]
                                Br, Cr = Brow2[par], Crow2[par]

                                @block.sync
                                def _(q, s16=s16, xd=xp_done, Br=Br, Cr=Cr):
                                    q.wait_ge(sv.h, xd)
                                    if s16 >= 2:
                                        q.wait_ge(st_.h, row_mm[s16 - 2])
                                    elif scan_prev[0]:
                                        q.wait_ge(st_.h, scan_prev[0])
                                    q.dma_start(out=Br[:],
                                                in_=Bsb[s16:s16 + 1, :]
                                                ).then_inc(sq.h, 16)
                                    q.dma_start(out=Cr[:],
                                                in_=Csb[s16:s16 + 1, :]
                                                ).then_inc(sq.h, 16)
                                sq.n += 32

                                @block.tensor
                                def _(t, s16=s16, sqn=sq.n, dtd=dt_half_done,
                                      Br=Br, Cr=Cr, ssn=ss.n):
                                    t.wait_ge(sq.h, sqn)
                                    t.wait_ge(sv.h, dtd)
                                    if s16 >= 2:
                                        t.wait_ge(ss.h, bc_ev[s16 - 2])
                                    else:
                                        t.wait_ge(ss.h, ssn)
                                    t.matmul(psf[4][:], c1xbf[:], Br[:],
                                             start=True, stop=True)
                                    t.matmul(psf[5][:], c1xbf[:], Cr[:],
                                             start=True,
                                             stop=True).then_inc(st_.h, 1)
                                st_.n += 1
                                row_mm[s16] = st_.n

                                @block.scalar
                                def _(s, s16=s16, stn=st_.n, Bq=Bq, Cq=Cq):
                                    s.wait_ge(st_.h, stn)
                                    if s16 >= 2:
                                        s.wait_ge(sv.h, sdone[s16 - 2])
                                    elif scan_prev[1]:
                                        s.wait_ge(sv.h, scan_prev[1])
                                    s.activation(Bq[:], psf[4][:], AF.Copy)
                                    s.activation(Cq[:], psf[5][:],
                                                 AF.Copy).then_inc(ss.h, 1)
                                ss.n += 1
                                bc_ev[s16] = ss.n
                                bc_ready = ss.n

                                for et in ets:
                                    eh = et % 8
                                    dslot = (et % 2)
                                    da = F3[:, dslot * 512:dslot * 512 + 512]

                                    @block.scalar
                                    def _(s, et=et, eh=eh, s16=s16, da=da,
                                          bcr=bc_ready, dslot=dslot):
                                        s.wait_ge(ss.h, bcr)
                                        if da_last[dslot]:
                                            s.wait_ge(sv.h, da_last[dslot])
                                        s.activation(da, dtv3[:, eh, :], AF.Exp,
                                                     scale=Asb[:, et * 16 + s16:
                                                               et * 16 + s16 + 1]
                                                     ).then_inc(ss.h, 1)
                                    ss.n += 1

                                    @block.vector
                                    def _(v, et=et, so=so, s16=s16, da=da,
                                          Bq=Bq, Cq=Cq, ssn=ss.n, svn=sv.n,
                                          gqn=gq.n, stn=st_.n):
                                        v.wait_ge(ss.h, ssn)
                                        v.wait_ge(gq.h, gqn)
                                        v.wait_ge(st_.h, stn)
                                        v.tensor_mul(dbx[:],
                                                     xs3[:, et, 4 + so:4 + so + 512],
                                                     Bq[:]).then_inc(sv.h, 1)
                                        v.wait_ge(sv.h, svn + 1)
                                        v.tensor_tensor_scan(
                                            hsl[et % 2][:], da, dbx[:],
                                            state[:, et * 16 + s16:
                                                  et * 16 + s16 + 1],
                                            ALU.mult, ALU.add).then_inc(sv.h, 1)
                                        v.wait_ge(sv.h, svn + 2)
                                        v.tensor_mul(psl[:], hsl[et % 2][:],
                                                     Cq[:]).then_inc(sv.h, 1)
                                    sv.n += 3
                                    da_last[dslot] = sv.n

                                    @block.gpsimd
                                    def _(g, et=et, s16=s16, svn=sv.n):
                                        g.wait_ge(sv.h, svn)
                                        g.tensor_copy(
                                            state[:, et * 16 + s16:
                                                  et * 16 + s16 + 1],
                                            hsl[et % 2][:, 511:512])
                                        g.tensor_add(
                                            B3[:, et * 512:(et + 1) * 512],
                                            B3[:, et * 512:(et + 1) * 512],
                                            psl[:]).then_inc(gq.h, 1)
                                    gq.n += 1
                                sdone[s16] = sv.n
                            scan_prev[0] = st_.n
                            scan_prev[1] = sv.n

                        scan_sv = sv.n
                        scan_gq = gq.n
                        for ep in range(8):
                            slot = wpc[ep % 2]

                            @block.gpsimd
                            def _(g, ep=ep, slot=slot, stn=st_.n,
                                  scv=scan_sv, scg=scan_gq):
                                g.wait_ge(st_.h, stn)
                                g.wait_ge(sv.h, scv)
                                g.wait_ge(gq.h, scg)
                                g.dma_start(
                                    out=slot.rearrange("p (dt e) -> p dt e",
                                                       dt=NDT),
                                    in_=inw[8 + ep]).then_inc(gq.h, 16)
                            gq.n += 16

                            @block.tensor
                            def _(t, ep=ep, so=so, slot=slot, gqn=gq.n,
                                  ssn=ss.n, svn=sv.n):
                                t.wait_ge(gq.h, gqn)
                                t.wait_ge(ss.h, ssn)
                                t.wait_ge(sv.h, svn)
                                src = (j0 + so if not is_rev
                                       else 2048 - j0 - so - 512)
                                for e2 in range(2):
                                    for dt in range(NDT):
                                        ins = t.matmul(
                                            psf[6 + e2][:],
                                            slot[:, dt * 256 + e2 * 128:
                                                 dt * 256 + e2 * 128 + 128],
                                            hh3[:, dt, src:src + 512],
                                            start=(dt == 0), stop=(dt == NDT - 1))
                                ins.then_inc(st_.h, 1)
                            st_.n += 1

                            if not is_rev:
                                @block.scalar
                                def _(s, ep=ep, stn=st_.n):
                                    s.wait_ge(st_.h, stn)
                                    ins = None
                                    for e2 in range(2):
                                        et = 2 * ep + e2
                                        ins = s.activation(
                                            B2[:, et * 512:(et + 1) * 512],
                                            psf[6 + e2][:], AF.Silu)
                                    ins.then_inc(ss.h, 1)
                                ss.n += 1
                            else:
                                @block.scalar
                                def _(s, ep=ep, stn=st_.n, svn=sv.n):
                                    s.wait_ge(st_.h, stn)
                                    s.wait_ge(sv.h, svn)
                                    s.activation(Bbb[:], psf[6][:], AF.Silu)
                                    s.activation(Cbb[:], psf[7][:],
                                                 AF.Silu).then_inc(ss.h, 1)
                                ss.n += 1

                                @block.vector
                                def _(v, ep=ep, ssn=ss.n):
                                    v.wait_ge(ss.h, ssn)
                                    et = 2 * ep
                                    v.tensor_copy(
                                        B2[:, et * 512:(et + 1) * 512][:, ::-1],
                                        Bbb[:])
                                    v.tensor_copy(
                                        B2[:, (et + 1) * 512:(et + 2) * 512][:, ::-1],
                                        Cbb[:]).then_inc(sv.h, 1)
                                sv.n += 1
                        zdone = (ss.n, sv.n)

                        for et in range(16):
                            @block.vector
                            def _(v, et=et, zd=zdone, gqn=gq.n, svn=sv.n):
                                v.wait_ge(ss.h, zd[0])
                                v.wait_ge(sv.h, max(zd[1], svn))
                                v.wait_ge(gq.h, gqn)
                                v.tensor_mul(B3[:, et * 512:(et + 1) * 512],
                                             B3[:, et * 512:(et + 1) * 512],
                                             B2[:, et * 512:(et + 1) * 512]
                                             ).then_inc(sv.h, 1)
                            sv.n += 1
                        yg_done = sv.n

                        op_mm = {}
                        for ct in range(16):
                            slot = wpc[ct % 2]

                            @block.gpsimd
                            def _(g, ct=ct, slot=slot, stn=st_.n):
                                if ct >= 2:
                                    g.wait_ge(st_.h, op_mm[ct - 2])
                                else:
                                    g.wait_ge(st_.h, stn)
                                g.dma_start(out=slot[:, 0:1024],
                                            in_=outw[ct]).then_inc(gq.h, 16)
                            gq.n += 16

                            @block.tensor
                            def _(t, ct=ct, slot=slot, gqn=gq.n, yd=yg_done):
                                t.wait_ge(gq.h, gqn)
                                t.wait_ge(sv.h, yd)
                                for ot in range(NDT):
                                    ins = t.matmul(
                                        psf[ot][:],
                                        slot[:, ot * 128:(ot + 1) * 128],
                                        B3[:, ct * 512:(ct + 1) * 512],
                                        start=(ct == 0), stop=(ct == 15))
                                ins.then_inc(st_.h, 1)
                            st_.n += 1
                            op_mm[ct] = st_.n

                        @block.vector
                        def _(v, c=c, so=so, stn=st_.n, rev=is_rev):
                            v.wait_ge(st_.h, stn)
                            jlo = c * 1024 + so
                            ins = None
                            for ot in range(NDT):
                                dst = fwdsl(ot, jlo, jlo + 512)
                                if not rev:
                                    ins = v.tensor_copy(dst, psf[ot][:])
                                else:
                                    ins = v.tensor_add(dst, dst, psf[ot][:])
                            ins.then_inc(sv.h, 1)
                        sv.n += 1
                    barrier()

            if "BNOM" not in stages:
                mamba("m1", False)
                if "BNOM2" not in stages:
                    mamba("m2", True)
            barrier()
            es_b.close()

            alloc_x("2")
            for half in range(2):
                @block.sync
                def _(q, half=half):
                    q.dma_start(out=X["sb"][:, half * 8192:(half + 1) * 8192],
                                in_=xspill.ap()[:, half * 8192:(half + 1) * 8192]
                                ).then_inc(sq.h, 16)
                sq.n += 16
            barrier()
            emit_norm(X["v"], norm_slots(), write_nb=False)
            barrier()

            if "BDUMPF" in stages:
                @block.gpsimd
                def _(g):
                    ov = out_h.ap().rearrange("(h g p) d -> h p g d", p=128, g=8)
                    g.dma_start(out=ov[0],
                                in_=B4[:].rearrange("p (g d) -> p g d", g=8)
                                ).then_inc(gq.h, 16)
                    g.dma_start(out=ov[1],
                                in_=B5[:].rearrange("p (g d) -> p g d", g=8)
                                ).then_inc(gq.h, 16)
                gq.n += 32
            if "BNOTAIL" in stages:
                return
            xb3 = B1[:].rearrange("p (dt t) -> p dt t", dt=NDT)
            for tch in range(NTC):
                o = tch * 512

                @block.tensor
                def _(t, o=o, svn=sv.n):
                    t.wait_ge(sv.h, svn)
                    t.matmul(psf[6][:], c1x[:], aB1[0:1, o:o + 512],
                             start=True, stop=True)
                    t.matmul(psf[7][:], c1x[:], bB1[0:1, o:o + 512],
                             start=True, stop=True).then_inc(st_.h, 1)
                st_.n += 1

                for dt in range(NDT):
                    @block.vector
                    def _(v, dt=dt, o=o, stn=st_.n, svn=sv.n, ssn=ss.n):
                        v.wait_ge(st_.h, stn)
                        v.wait_ge(ss.h, ssn)
                        v.tensor_mul(F1[:, 0:512], X["v"][:, dt, o:o + 512],
                                     psf[6][:]).then_inc(sv.h, 1)
                        v.wait_ge(sv.h, svn + 1)
                        v.tensor_add(F1[:, 0:512], F1[:, 0:512],
                                     psf[7][:]).then_inc(sv.h, 1)
                        v.wait_ge(sv.h, svn + 2)
                        v.tensor_add(F3[:, 512:1024], F1[:, 0:512],
                                     fwdsl(dt, o, o + 512)).then_inc(sv.h, 1)
                    sv.n += 3

                    @block.scalar
                    def _(s, svn=sv.n):
                        s.wait_ge(sv.h, svn)
                        s.activation(F1[:, 512:1024], F3[:, 512:1024],
                                     AF.Gelu).then_inc(ss.h, 1)
                    ss.n += 1

                    @block.vector
                    def _(v, dt=dt, o=o, ssn=ss.n, svn=sv.n):
                        v.wait_ge(ss.h, ssn)
                        v.tensor_add(xb3[:, dt, o:o + 512], F1[:, 0:512],
                                     F1[:, 512:1024]).then_inc(sv.h, 1)
                        v.wait_ge(sv.h, svn + 1)
                        v.tensor_add(X["v"][:, dt, o:o + 512],
                                     X["v"][:, dt, o:o + 512],
                                     xb3[:, dt, o:o + 512]).then_inc(sv.h, 1)
                    sv.n += 2
            barrier()

            if "BNOFFN" in stages:
                return
            hfm = {}
            for dt in range(NDT):
                pool = B2 if dt < 4 else B3
                hfm[dt] = pool[:, (dt % 4) * 2048:((dt % 4) + 1) * 2048]

            class HfShim:
                def __getitem__(self, key):
                    _, dt, tsl = key
                    return hfm[dt][:, tsl]
            hfw = HfShim()
            emit_norm(xb3, norm_slots(), write_nb=True, affine=(fgc, fbc),
                      dst_view=hfw, ones_col=cColbf)
            barrier()
            if "BDUMPH" in stages:
                @block.gpsimd
                def _(g):
                    ov = out_h.ap().rearrange("(h g p) d -> h p g d", p=128, g=8)
                    g.dma_start(out=ov[0],
                                in_=B2[:].rearrange("p (g d) -> p g d", g=8)
                                ).then_inc(gq.h, 16)
                    g.dma_start(out=ov[1],
                                in_=B3[:].rearrange("p (g d) -> p g d", g=8)
                                ).then_inc(gq.h, 16)
                gq.n += 32
                return

            for dt in range(NDT):
                @block.vector
                def _(v, dt=dt, sqn=sq.n, svn=sv.n):
                    v.wait_ge(sq.h, sqn)
                    v.wait_ge(sv.h, svn)
                    v.tensor_scalar(X["v"][:, dt, :], X["v"][:, dt, :],
                                    f2bsb[:, dt:dt + 1], None,
                                    ALU.add).then_inc(sv.h, 1)
                sv.n += 1

            ff1a = inp["b_ff1t"].ap()
            ff2a = inp["b_ff2T"].ap().rearrange("(fp p) o -> fp p o", p=128)
            fp_mm = {}
            for fp in range(32):
                slot1 = W5[:, (fp % 2) * 1024:(fp % 2) * 1024 + 1024]
                slot2 = W5[:, 2048 + (fp % 2) * 1024:2048 + (fp % 2) * 1024 + 1024]

                @block.gpsimd
                def _(g, fp=fp, slot1=slot1, slot2=slot2, stn=st_.n):
                    if fp >= 2:
                        g.wait_ge(st_.h, fp_mm[fp - 2])
                    else:
                        g.wait_ge(st_.h, stn)
                    g.dma_start(out=slot1.rearrange("p (dt f) -> p dt f", dt=NDT),
                                in_=ff1a[fp]).then_inc(gq.h, 16)
                    g.dma_start(out=slot2[:, 0:1024],
                                in_=ff2a[fp]).then_inc(gq.h, 16)
                gq.n += 32

                for tc in range(NTC):
                    o = tc * 512

                    @block.tensor
                    def _(t, fp=fp, o=o, slot1=slot1, gqn=gq.n, svn=sv.n):
                        t.wait_ge(gq.h, gqn)
                        t.wait_ge(sv.h, svn)
                        for dt in range(NDT):
                            ins = t.matmul(psf[6][:],
                                           slot1[:, dt * 128:(dt + 1) * 128],
                                           hfw[:, dt, o:o + 512],
                                           start=(dt == 0), stop=(dt == NDT - 1))
                        ins.then_inc(st_.h, 1)
                    st_.n += 1

                    @block.scalar
                    def _(s, fp=fp, stn=st_.n, ssn=ss.n):
                        s.wait_ge(st_.h, stn)
                        s.wait_ge(ss.h, ssn)
                        s.activation(W5[:, 4160:4672], psf[6][:], AF.Gelu,
                                     bias=f1bsb[:, fp:fp + 1]).then_inc(ss.h, 1)
                    ss.n += 1

                    @block.tensor
                    def _(t, fp=fp, slot2=slot2, ssn=ss.n, svn=sv.n):
                        t.wait_ge(ss.h, ssn)
                        t.wait_ge(sv.h, svn)
                        for ot in range(NDT):
                            ins = t.matmul(psf[ot][:],
                                           slot2[:, ot * 128:(ot + 1) * 128],
                                           W5[:, 4160:4672],
                                           start=True, stop=True)
                        ins.then_inc(st_.h, 1)
                    st_.n += 1
                    fp_mm[fp] = st_.n

                    @block.vector
                    def _(v, o=o, stn=st_.n, svn=sv.n):
                        v.wait_ge(st_.h, stn)
                        for ot in range(NDT):
                            v.wait_ge(sv.h, svn + ot)
                            v.tensor_add(X["v"][:, ot, o:o + 512],
                                         X["v"][:, ot, o:o + 512],
                                         psf[ot][:]).then_inc(sv.h, 1)
                    sv.n += NDT
            barrier()

        def phase_store():
            cp_done = {}
            store_at = {}
            for tt in range(16):
                land = F1 if tt % 2 == 0 else F3

                @block.tensor
                def _(t, tt=tt):
                    if tt >= 1:
                        t.wait_ge(sv.h, cp_done[tt - 1])
                    for dt in range(NDT):
                        t.transpose(psf[dt][:, 0:128],
                                    X["v"][:, dt, tt * 128:(tt + 1) * 128],
                                    cI[:]).then_inc(st_.h, 1)
                st_.n += NDT

                @block.vector
                def _(v, tt=tt, land=land, base=st_.n - NDT):
                    if tt >= 2:
                        v.wait_ge(sq.h, store_at[tt - 2])
                    for dt in range(NDT):
                        v.wait_ge(st_.h, base + dt + 1)
                        v.tensor_copy(land[:, dt * 128:(dt + 1) * 128],
                                      psf[dt][:, 0:128]).then_inc(sv.h, 1)
                sv.n += NDT
                cp_done[tt] = sv.n

                @block.sync
                def _(q, tt=tt, land=land, svn=sv.n):
                    q.wait_ge(sv.h, svn)
                    q.dma_start(out=out_h.ap()[tt * 128:(tt + 1) * 128, :],
                                in_=land[:, 0:1024]).then_inc(sq.h, 16)
                sq.n += 16
                store_at[tt] = sq.n

            @block.sync
            def _(q, sqn=sq.n):
                q.wait_ge(sq.h, sqn)

        phase_load()
        if "A" in stages:
            phase_attn_a()
        if "M1" in stages:
            phase_moe()
        if "B" in stages:
            phase_attn_b()
        if "M2" in stages:
            phase_moe()
        if "NOSTORE" in stages:
            @block.sync
            def _(q, sqn=sq.n, gqn=gq.n):
                q.wait_ge(sq.h, sqn)
                q.wait_ge(gq.h, gqn)
        else:
            phase_store()

    es_x.close()
    es.close()
    return nc


def make_in_maps(inputs):
    rn = {k: np.asarray(v) for k, v in inputs.items()}
    x = np.ascontiguousarray(rn["L0_attn_a_inputs"], dtype=np.float32)

    def t(a):
        return np.ascontiguousarray(np.asarray(a).T)

    base = {
        "aWQ": rn["L0_attn_a_WQ"], "aWK": rn["L0_attn_a_WK"],
        "aWVd": rn["L0_attn_a_WV_down"], "aWVu": rn["L0_attn_a_WV_up"],
        "moe_w1x": np.ascontiguousarray(
            rn["L0_ffn_w1"][:, :, :4096].reshape(NE, NDT, 128, 32, 128)
            .transpose(0, 3, 2, 1, 4)),
        "moe_w1g": np.ascontiguousarray(
            rn["L0_ffn_w1"][:, :, 4096:].reshape(NE, NDT, 128, 32, 128)
            .transpose(0, 3, 2, 1, 4)),
        "moe_w2t": np.ascontiguousarray(
            rn["L0_ffn_w2"].reshape(NE, 16, 2, 128, D).transpose(0, 1, 3, 2, 4)),
        "moe_b1c": np.ascontiguousarray(
            rn["L0_ffn_b1"].reshape(NE, 64, 128).transpose(0, 2, 1)),
        "moe_b2c": np.ascontiguousarray(
            rn["L0_ffn_b2"].reshape(NE, 8, 128).transpose(0, 2, 1)),
        "moe_gwr": np.ascontiguousarray(rn["L0_ffn_gate_w"].reshape(NDT, 128, NE)),
        "moe_gbc": np.ascontiguousarray(rn["L0_ffn_gate_b"].reshape(NE, 1)),
        "b_ng": t(rn["L0_attn_b_norm_g"].reshape(NDT, 128)),
        "b_nb": t(rn["L0_attn_b_norm_b"].reshape(NDT, 128)),
        "b_fg": t(rn["L0_attn_b_ffn_g"].reshape(NDT, 128)),
        "b_fb": t(rn["L0_attn_b_ffn_b"].reshape(NDT, 128)),
        "b_ff1t": np.ascontiguousarray(
            t(rn["L0_attn_b_ff1_w"]).reshape(NDT, 128, 32, 128)
            .transpose(2, 1, 0, 3)),
        "b_ff1b": t(rn["L0_attn_b_ff1_b"].reshape(32, 128)),
        "b_ff2T": t(rn["L0_attn_b_ff2_w"]),
        "b_ff2b": t(rn["L0_attn_b_ff2_b"].reshape(NDT, 128)),
    }
    for m in ("m1", "m2"):
        base[f"b_{m}_inwt"] = np.ascontiguousarray(
            t(rn[f"L0_attn_b_{m}_in_w"]).reshape(NDT, 128, 16, 256)
            .transpose(2, 1, 0, 3))
        base[f"b_{m}_outT"] = t(rn[f"L0_attn_b_{m}_out_w"])
        base[f"b_{m}_xprojT"] = t(rn[f"L0_attn_b_{m}_xproj_w"])
        base[f"b_{m}_dtT"] = t(rn[f"L0_attn_b_{m}_dt_w"])
        base[f"b_{m}_cw"] = np.ascontiguousarray(
            rn[f"L0_attn_b_{m}_conv_w"].reshape(NIT, 128, 4))
        base[f"b_{m}_cb"] = t(rn[f"L0_attn_b_{m}_conv_b"].reshape(NIT, 128))
        base[f"b_{m}_dtb"] = t(rn[f"L0_attn_b_{m}_dt_b"].reshape(NIT, 128))
        base[f"b_{m}_Alog"] = np.ascontiguousarray(
            rn[f"L0_attn_b_{m}_Alog"].reshape(NIT, 128, DST))
        base[f"b_{m}_D"] = t(rn[f"L0_attn_b_{m}_D"].reshape(NIT, 128))
    for k, v in host_constants().items():
        base[k] = v
    base = {k: np.ascontiguousarray(v, dtype=np.float32) for k, v in base.items()}
    maps = []
    for i in range(NCORES):
        m = dict(base)
        m["xcore"] = np.ascontiguousarray(x[i])
        maps.append(m)
    return maps


def kernel(**inputs):
    nc = build_program()
    in_maps = make_in_maps(inputs)
    res = run_bass_kernel_spmd(nc, in_maps, core_ids=list(range(NCORES)))
    out = np.stack([np.asarray(r["out"]) for r in res.results], axis=0)
    return out.astype(np.float32)


# version 05 (final)
# speedup vs baseline: 1.2242x; 1.2242x over previous
import contextlib
import numpy as np
import ml_dtypes
import concourse.bass as bass
import concourse.mybir as mybir
from concourse.bass_utils import run_bass_kernel_spmd

F32 = mybir.dt.float32
BF16 = mybir.dt.bfloat16
AF = mybir.ActivationFunctionType
ALU = mybir.AluOpType
AX = mybir.AxisListType

NCORES = 8
T = 2048
D = 1024
NDT = D // 128
NTC = T // 512
NH = 16
DH = 64
DI = 2048
NIT = DI // 128
DST = 16
DTR = 64
NE = 8
DFFE = 8192
NEG = -3.0e38
EPS = 1e-5

STAGES_ALL = ("A", "M1", "B", "M2")


class Sem:
    def __init__(self, h):
        self.h = h
        self.n = 0


def host_constants():
    c = {}
    c["ident_f32"] = np.eye(128, dtype=np.float32)
    c["ones_1x128"] = np.ones((1, 128), dtype=np.float32)
    c["ones_1x128_s16"] = np.full((1, 128), 1.0 / 16.0, dtype=np.float32)
    c["ones_128x1"] = np.ones((128, 1), dtype=np.float32)
    cv = np.zeros((128, 8), dtype=np.float32)
    cv[:, 0] = EPS
    cv[:, 1] = 1.0
    c["cvals"] = cv
    p = np.arange(128)[:, None]
    f = np.arange(512)[None, :]
    mb = np.zeros((128, 4, 512), dtype=np.float32)
    dm = np.zeros((128, 4, 512), dtype=np.float32)
    for r in range(4):
        mb[:, r, :] = np.where(f >= 128 * r + p, 0.0, NEG)
        dm[:, r, :] = np.where(f == 128 * r + p, 1.0, 0.0)
    c["maskdiag"] = np.concatenate([mb.reshape(128, 2048), dm.reshape(128, 2048)],
                                   axis=1)
    return c


def build_program(stages=STAGES_ALL):
    nc = bass.Bass("TRN2", target_bir_lowering=False)
    es = contextlib.ExitStack()

    inp = {}

    def dri(name, shape):
        inp[name] = nc.dram_tensor(name, shape, F32, kind="ExternalInput")
        return inp[name]

    x_h = dri("xcore", [T, D])
    wq_h = dri("aWQ", [NH, D, DH])
    wk_h = dri("aWK", [NH, D, DH])
    wvd_h = dri("aWVd", [NH, D, DH])
    wvu_h = dri("aWVu", [NH, DH, D])
    for m in ("m1", "m2"):
        dri(f"b_{m}_inwt", [16, 128, NDT, 256])
        dri(f"b_{m}_outT", [DI, D])
        dri(f"b_{m}_xprojT", [DI, DTR + 2 * DST])
        dri(f"b_{m}_dtT", [DTR, DI])
        dri(f"b_{m}_cw", [NIT, 128, 4])
        dri(f"b_{m}_cb", [128, NIT])
        dri(f"b_{m}_dtb", [128, NIT])
        dri(f"b_{m}_Alog", [NIT, 128, DST])
        dri(f"b_{m}_D", [128, NIT])
    dri("b_ng", [128, NDT])
    dri("b_nb", [128, NDT])
    dri("b_fg", [128, NDT])
    dri("b_fb", [128, NDT])
    dri("b_ff1t", [32, 128, NDT, 128])
    dri("b_ff1b", [128, 32])
    dri("b_ff2T", [4 * D, D])
    dri("b_ff2b", [128, 8])
    dri("moe_w1x", [NE, 32, 128, NDT, 128])
    dri("moe_w1g", [NE, 32, 128, NDT, 128])
    dri("moe_w2t", [NE, 16, 128, 2, D])
    dri("moe_b1c", [NE, 128, 64])
    dri("moe_b2c", [NE, 128, 8])
    dri("moe_gwr", [NDT, 128, NE])
    dri("moe_gbc", [NE, 1])

    hc = host_constants()
    for k, v in hc.items():
        inp[k] = nc.dram_tensor(k, list(v.shape), F32, kind="ExternalInput")

    out_h = nc.dram_tensor("out", [T, D], F32, kind="ExternalOutput")
    flat_hb = nc.dram_tensor("flat_hb", [1, 4096], F32, kind="Internal")
    coef_hb = nc.dram_tensor("coef_hb", [8, 1024], F32, kind="Internal")
    xspill = nc.dram_tensor("xspill", [128, NDT * T], F32, kind="Internal")

    def sbuf(name, shape, dtype):
        return es.enter_context(nc.sbuf_tensor(name, shape, dtype))

    B1 = sbuf("B1", [128, 16384], BF16)
    B2 = sbuf("B2", [128, 8192], BF16)
    B3 = sbuf("B3", [128, 8192], BF16)
    B4 = sbuf("B4", [128, 8192], BF16)
    B5 = sbuf("B5", [128, 8192], BF16)
    W5 = sbuf("W5", [128, 5632], BF16)
    F1 = sbuf("F1", [128, 1024], F32)
    F3 = sbuf("F3", [128, 1024], F32)
    vec = sbuf("vec", [1, 4096], F32)
    vec2 = sbuf("vec2", [1, 2048], F32)
    sm1 = sbuf("sm1", [128, 40], F32)
    flat16 = sbuf("flat16", [16, 128], F32)
    cI = sbuf("cI", [128, 128], F32)
    c1x = sbuf("c1x", [1, 128], F32)
    c1x16 = sbuf("c1x16", [1, 128], F32)
    cCol = sbuf("cCol", [128, 1], F32)
    cVal = sbuf("cVal", [128, 8], F32)
    gbc = sbuf("gbc", [8, 1], F32)
    b1sb = sbuf("b1sb", [128, 64], F32)
    b2sb = sbuf("b2sb", [128, 8], F32)
    ngc = sbuf("ngc", [128, 8], F32)
    nbc2 = sbuf("nbc2", [128, 8], F32)
    fgc = sbuf("fgc", [128, 8], F32)
    fbc = sbuf("fbc", [128, 8], F32)
    f1bsb = sbuf("f1bsb", [128, 32], F32)
    f2bsb = sbuf("f2bsb", [128, 8], F32)
    cColbf = sbuf("cColbf", [128, 1], BF16)
    c1xbf = sbuf("c1xbf", [1, 128], BF16)

    es_x = contextlib.ExitStack()
    X = {}

    def alloc_x(tag=""):
        X["sb"] = es_x.enter_context(nc.sbuf_tensor("x" + tag, [128, NDT * T], F32))
        X["v"] = X["sb"][:].rearrange("p (dt t) -> p dt t", dt=NDT)

    alloc_x()

    psf = [es.enter_context(nc.psum_tensor(f"psf{i}", [128, 512], F32))
           for i in range(8)]

    sq = Sem(es.enter_context(nc.semaphore("sq")))
    gq = Sem(es.enter_context(nc.semaphore("gq")))
    st_ = Sem(es.enter_context(nc.semaphore("st")))
    sv = Sem(es.enter_context(nc.semaphore("sv")))
    ss = Sem(es.enter_context(nc.semaphore("ss")))
    ph = Sem(es.enter_context(nc.semaphore("ph")))
    nbar = [0]

    nbv = B1[:].rearrange("p (dt t) -> p dt t", dt=NDT)

    with nc.Block() as block:

        def barrier():
            nbar[0] += 1
            tgt = 5 * nbar[0]
            sqn, gqn, stn, svn, ssn = sq.n, gq.n, st_.n, sv.n, ss.n

            @block.tensor
            def _(t):
                t.wait_ge(st_.h, stn)
                t.sem_inc(ph.h, 1)
                t.wait_ge(ph.h, tgt)

            @block.vector
            def _(v):
                v.wait_ge(sv.h, svn)
                v.sem_inc(ph.h, 1)
                v.wait_ge(ph.h, tgt)

            @block.scalar
            def _(s):
                s.wait_ge(ss.h, ssn)
                s.sem_inc(ph.h, 1)
                s.wait_ge(ph.h, tgt)

            @block.gpsimd
            def _(g):
                g.wait_ge(gq.h, gqn)
                g.sem_inc(ph.h, 1)
                g.wait_ge(ph.h, tgt)

            @block.sync
            def _(q):
                q.wait_ge(sq.h, sqn)
                q.sem_inc(ph.h, 1)
                q.wait_ge(ph.h, tgt)

        def phase_load():
            @block.sync
            def _(q):
                q.dma_start(out=cI[:], in_=inp["ident_f32"].ap()).then_inc(sq.h, 16)
                q.dma_start(out=c1x[:], in_=inp["ones_1x128"].ap()).then_inc(sq.h, 16)
                q.dma_start(out=c1x16[:], in_=inp["ones_1x128_s16"].ap()).then_inc(sq.h, 16)
                q.dma_start(out=cCol[:], in_=inp["ones_128x1"].ap()).then_inc(sq.h, 16)
                q.dma_start(out=cVal[:], in_=inp["cvals"].ap()).then_inc(sq.h, 16)
                q.dma_start(out=gbc[:], in_=inp["moe_gbc"].ap()).then_inc(sq.h, 16)
            sq.n += 96

            @block.gpsimd
            def _(g):
                g.dma_start(out=cColbf[:],
                            in_=inp["ones_128x1"].ap()).then_inc(sq.h, 16)
                g.dma_start(out=c1xbf[:],
                            in_=inp["ones_1x128"].ap()).then_inc(sq.h, 16)
            sq.n += 32
            consts_ld = sq.n

            xh = x_h.ap().rearrange("(tt p) d -> tt p d", p=128)
            tr_done = {}
            cp_done = {}
            for tt in range(16):
                land = F1 if tt % 2 == 0 else F3

                @block.gpsimd
                def _(g, tt=tt, land=land):
                    if tt >= 2:
                        g.wait_ge(st_.h, tr_done[tt - 2])
                    g.dma_start(out=land[:, 0:1024], in_=xh[tt]).then_inc(gq.h, 16)
                gq.n += 16

                @block.tensor
                def _(t, tt=tt, land=land, gqn=gq.n):
                    t.wait_ge(gq.h, gqn)
                    if tt == 0:
                        t.wait_ge(sq.h, consts_ld)
                    if tt >= 1:
                        t.wait_ge(sv.h, cp_done[tt - 1])
                    for dt in range(NDT):
                        t.transpose(psf[dt][:, 0:128],
                                    land[:, dt * 128:(dt + 1) * 128],
                                    cI[:]).then_inc(st_.h, 1)
                st_.n += NDT
                tr_done[tt] = st_.n

                @block.vector
                def _(v, tt=tt, base=st_.n - NDT):
                    for dt in range(NDT):
                        v.wait_ge(st_.h, base + dt + 1)
                        v.tensor_copy(X["v"][:, dt, tt * 128:(tt + 1) * 128],
                                      psf[dt][:, 0:128]).then_inc(sv.h, 1)
                sv.n += NDT
                cp_done[tt] = sv.n

            barrier()

        def emit_norm(src_view, sl, write_nb=True, affine=None,
                      fold_residual=False, dst_view=None, ones_col=None):
            if dst_view is None:
                dst_view = nbv
            oc = cCol if ones_col is None else ones_col
            vmu, vvar, vtmp = sl["mu"], sl["var"], sl["tmp"]
            vaB, vbB = sl["aB"], sl["bB"]
            vaB1 = sl.get("aB1")
            stats_done = {}
            cps = {}
            for tch in range(NTC):
                for qr in range(4):
                    @block.scalar
                    def _(s, tch=tch, qr=qr):
                        prev = (tch, qr - 1) if qr else (tch - 1, 3)
                        if prev in stats_done:
                            s.wait_ge(st_.h, stats_done[prev])
                        for k in range(2):
                            dt = qr * 2 + k
                            ins = s.activation(F1[:, k * 512:(k + 1) * 512],
                                               src_view[:, dt, tch * 512:(tch + 1) * 512],
                                               AF.Square)
                        ins.then_inc(ss.h, 1)
                    ss.n += 1

                    @block.tensor
                    def _(t, tch=tch, qr=qr, ssn=ss.n):
                        t.wait_ge(ss.h, ssn)
                        if qr == 0 and tch >= 1:
                            t.wait_ge(sv.h, cps[tch - 1])
                        for k in range(2):
                            dt = qr * 2 + k
                            t.matmul(psf[4][0:1, :], oc[:],
                                     src_view[:, dt, tch * 512:(tch + 1) * 512],
                                     start=(dt == 0), stop=(dt == NDT - 1))
                            ins = t.matmul(psf[5][0:1, :], cCol[:],
                                           F1[:, k * 512:(k + 1) * 512],
                                           start=(dt == 0), stop=(dt == NDT - 1))
                        ins.then_inc(st_.h, 1)
                    st_.n += 1
                    stats_done[(tch, qr)] = st_.n

                @block.vector
                def _(v, tch=tch, stn=st_.n):
                    v.wait_ge(st_.h, stn)
                    o = tch * 512
                    v.tensor_scalar_mul(vmu[0:1, o:o + 512], psf[4][0:1, :], 1.0 / D)
                    v.tensor_copy(vvar[0:1, o:o + 512],
                                  psf[5][0:1, :]).then_inc(sv.h, 1)
                sv.n += 1
                cps[tch] = sv.n

            @block.vector
            def _(v):
                v.tensor_mul(vtmp, vmu, vmu)
                v.scalar_tensor_tensor(vvar, vvar, 1.0 / D, vtmp,
                                       ALU.mult, ALU.subtract).then_inc(sv.h, 1)
            sv.n += 1

            @block.scalar
            def _(s, svn=sv.n):
                s.wait_ge(sv.h, svn)
                s.activation(vtmp, vvar, AF.Ln, bias=cVal[0:1, 0:1])
                s.activation(vaB, vtmp, AF.Exp, scale=-0.5).then_inc(ss.h, 1)
            ss.n += 1

            @block.vector
            def _(v, ssn=ss.n):
                v.wait_ge(ss.h, ssn)
                ins = v.scalar_tensor_tensor(vbB, vmu, -1.0, vaB, ALU.mult, ALU.mult)
                if fold_residual:
                    ins = v.tensor_scalar_add(vaB1, vaB, 1.0)
                ins.then_inc(sv.h, 1)
            sv.n += 1
            coeffs_done = sv.n

            if not write_nb:
                return

            ap_done = {}
            for tch in range(NTC):
                @block.tensor
                def _(t, tch=tch):
                    t.wait_ge(sv.h, coeffs_done)
                    if tch >= 1:
                        t.wait_ge(sv.h, ap_done[tch - 1])
                    o = tch * 512
                    t.matmul(psf[6][:], c1x[:], vaB[0:1, o:o + 512],
                             start=True, stop=True)
                    ins = t.matmul(psf[7][:], c1x[:], vbB[0:1, o:o + 512],
                                   start=True, stop=True)
                    if fold_residual:
                        ins = t.matmul(psf[4][:], c1x[:], vaB1[0:1, o:o + 512],
                                       start=True, stop=True)
                    ins.then_inc(st_.h, 1)
                st_.n += 1

                @block.vector
                def _(v, tch=tch, stn=st_.n):
                    v.wait_ge(st_.h, stn)
                    o = tch * 512
                    for dt in range(NDT):
                        v.tensor_mul(F3[:, 0:512], src_view[:, dt, o:o + 512], psf[6][:])
                        if affine is None:
                            ins = v.tensor_add(dst_view[:, dt, o:o + 512],
                                               F3[:, 0:512], psf[7][:])
                        else:
                            v.tensor_add(F3[:, 512:1024], F3[:, 0:512], psf[7][:])
                            g_sb, b_sb = affine
                            ins = v.tensor_scalar(dst_view[:, dt, o:o + 512],
                                                  F3[:, 512:1024], g_sb[:, dt:dt + 1],
                                                  b_sb[:, dt:dt + 1], ALU.mult, ALU.add)
                        if fold_residual:
                            v.tensor_mul(F3[:, 512:1024],
                                         src_view[:, dt, o:o + 512], psf[4][:])
                            ins = v.tensor_add(src_view[:, dt, o:o + 512],
                                               F3[:, 512:1024], psf[7][:])
                        ins.then_inc(sv.h, 1)
                sv.n += NDT
                ap_done[tch] = sv.n

        def norm_slots():
            return dict(mu=vec[0:1, 0:2048], var=vec[0:1, 2048:4096],
                        tmp=vec2[0:1, :], aB=vec2[0:1, :],
                        bB=vec[0:1, 2048:4096], aB1=vec[0:1, 0:2048])

        def phase_attn_a():
            emit_norm(X["v"], norm_slots(), write_nb=True, fold_residual=True)
            barrier()

            @block.gpsimd
            def _(g):
                g.dma_start(out=B4[:, 4096:6144],
                            in_=inp["maskdiag"].ap()[:, 0:2048]).then_inc(gq.h, 16)
                g.dma_start(out=B3[:, 6144:8192],
                            in_=inp["maskdiag"].ap()[:, 2048:4096]).then_inc(gq.h, 16)
            gq.n += 32
            masks_ld = gq.n

            q2 = B2[:, 0:2048]
            k2 = B2[:, 2048:4096]
            vd2 = B2[:, 4096:6144]
            vds = B2[:, 6144:8192]
            wq_b = B3[:, 0:2048]
            wk_b = B4[:, 0:2048]
            wvd_b = B5[:, 0:2048]
            wvu_sb = B3[:, 4096:5120]
            attn16 = sm1[:, 16:32]
            csums = sm1[:, 32:40]
            mchunk = F3[:, 0:512]
            junk = F3[:, 512:1024]
            flatv = vec[0:1, 0:2048]

            pair_w = {}
            pair_done = {}
            a_tr_done = 0

            for hp in range(8):
                bo = (hp % 2) * 1024

                @block.gpsimd
                def _(g, hp=hp, bo=bo):
                    if hp >= 2:
                        g.wait_ge(st_.h, pair_done[hp - 2])
                    for wsb, wh in ((wq_b, wq_h), (wk_b, wk_h), (wvd_b, wvd_h)):
                        for n01 in range(2):
                            n = 2 * hp + n01
                            for dt in range(NDT):
                                o = bo + dt * 128 + n01 * 64
                                g.dma_start(out=wsb[:, o:o + 64],
                                            in_=wh.ap()[n, dt * 128:(dt + 1) * 128, :]
                                            ).then_inc(gq.h, 16)
                gq.n += 48 * 16

                @block.tensor
                def _(t, hp=hp, bo=bo, gqn=gq.n, svn=sv.n, ssn=ss.n):
                    t.wait_ge(gq.h, gqn)
                    t.wait_ge(sv.h, svn)
                    t.wait_ge(ss.h, ssn)
                    if hp == 0:
                        t.wait_ge(gq.h, masks_ld)
                    for wi, wsb in enumerate((wq_b, wk_b, wvd_b)):
                        for tch in range(NTC):
                            u = wi * NTC + tch
                            pb = psf[u % 6]
                            if u >= 6:
                                t.wait_ge(sv.h, svn + u - 6 + 1)
                            for dt in range(NDT):
                                ins = t.matmul(pb[:],
                                               wsb[:, bo + dt * 128:bo + (dt + 1) * 128],
                                               nbv[:, dt, tch * 512:(tch + 1) * 512],
                                               start=(dt == 0), stop=(dt == NDT - 1))
                            ins.then_inc(st_.h, 1)
                st_.n += 3 * NTC

                @block.vector
                def _(v, base=st_.n - 3 * NTC):
                    for wi, dst in enumerate((q2, k2, vd2)):
                        for tch in range(NTC):
                            v.wait_ge(st_.h, base + wi * NTC + tch + 1)
                            pb = psf[(wi * NTC + tch) % 6]
                            if wi == 1:
                                v.tensor_scalar_mul(dst[:, tch * 512:(tch + 1) * 512],
                                                    pb[:], 0.125).then_inc(sv.h, 1)
                            else:
                                v.tensor_copy(dst[:, tch * 512:(tch + 1) * 512],
                                              pb[:]).then_inc(sv.h, 1)
                sv.n += 3 * NTC
                proj_ev = sv.n

                sc_ev = {0: ss.n, 1: ss.n, 2: ss.n, 3: ss.n}
                for n01 in range(2):
                    plo = n01 * 64
                    for si in range(16):
                        j0, r = si // 4, si % 4
                        nchunks = 4 - j0
                        mm_at = {}
                        for ci in range(nchunks):
                            j = j0 + ci

                            @block.tensor
                            def _(t, si=si, j=j, ci=ci, plo=plo, pe=proj_ev,
                                  scv=sc_ev[ci % 4]):
                                t.wait_ge(sv.h, pe)
                                t.wait_ge(ss.h, scv)
                                t.matmul(psf[ci][:],
                                         k2[plo:plo + 64, si * 128:(si + 1) * 128],
                                         q2[plo:plo + 64, j * 512:(j + 1) * 512],
                                         start=True, stop=True).then_inc(st_.h, 1)
                            st_.n += 1
                            mm_at[j] = st_.n

                        svn_pre = sv.n

                        @block.vector
                        def _(v, r=r, mm0=mm_at[j0], ssn=ss.n, trd=a_tr_done,
                              svn_pre=sv.n):
                            v.wait_ge(ss.h, ssn)
                            v.wait_ge(st_.h, max(mm0, trd))
                            mb = B4[:, 4096 + r * 512:4096 + (r + 1) * 512]
                            dmk = B3[:, 6144 + r * 512:6144 + (r + 1) * 512]
                            v.scalar_tensor_tensor(junk[:, 0:512], psf[0][:], 1.0,
                                                   dmk, ALU.bypass, ALU.mult,
                                                   accum_out=sm1[:, 0:1]).then_inc(sv.h, 1)
                            v.scalar_tensor_tensor(mchunk, psf[0][:], 1.0, mb,
                                                   ALU.bypass, ALU.add)
                            v.wait_ge(sv.h, svn_pre + 1)
                            v.tensor_scalar_mul(sm1[:, 1:2], sm1[:, 0:1],
                                                -1.0).then_inc(sv.h, 1)
                        sv.n += 2
                        mask_done = sv.n

                        @block.scalar
                        def _(s, j0=j0, nchunks=nchunks, md=mask_done,
                              mm_at=dict(mm_at)):
                            s.wait_ge(sv.h, md)
                            ins = s.activation(junk[:, 0:512], mchunk, AF.Exp,
                                               bias=sm1[:, 1:2], accum_out=csums[:, 0:1])
                            for ci in range(1, nchunks):
                                s.wait_ge(st_.h, mm_at[j0 + ci])
                                ins = s.activation(junk[:, 0:512], psf[ci][:], AF.Exp,
                                                   bias=sm1[:, 1:2],
                                                   accum_out=csums[:, ci:ci + 1])
                            ins.then_inc(ss.h, 1)
                        ss.n += 1
                        for ci in range(nchunks):
                            sc_ev[ci] = ss.n

                        @block.vector
                        def _(v, si=si, nchunks=nchunks, ssn=ss.n, svn_pre=sv.n):
                            v.wait_ge(ss.h, ssn)
                            v.tensor_reduce(sm1[:, 2:3], csums[:, 0:nchunks],
                                            AX.X, ALU.add).then_inc(sv.h, 1)
                            v.wait_ge(sv.h, svn_pre + 1)
                            v.reciprocal(attn16[:, si:si + 1],
                                         sm1[:, 2:3]).then_inc(sv.h, 1)
                        sv.n += 2

                    @block.tensor
                    def _(t, svn=sv.n, scv=dict(sc_ev)):
                        t.wait_ge(sv.h, svn)
                        t.wait_ge(ss.h, max(scv.values()))
                        t.transpose(psf[4][0:16, 0:128], attn16,
                                    cI[:]).then_inc(st_.h, 1)
                    st_.n += 1
                    a_tr_done = st_.n

                    @block.vector
                    def _(v, stn=st_.n):
                        v.wait_ge(st_.h, stn)
                        v.tensor_copy(flat16[0:16, 0:128],
                                      psf[4][0:16, 0:128]).then_inc(sv.h, 1)
                    sv.n += 1

                    @block.sync
                    def _(q, svn=sv.n, sqn=sq.n, stn=st_.n):
                        q.wait_ge(sv.h, svn)
                        q.wait_ge(st_.h, stn)
                        q.dma_start(out=flat_hb.ap()[0:1, 0:2048]
                                    .rearrange("o (s p) -> (o s) p", p=128),
                                    in_=flat16[0:16, 0:128]).then_inc(sq.h, 16)
                        q.wait_ge(sq.h, sqn + 16)
                        q.dma_start(out=flatv,
                                    in_=flat_hb.ap()[0:1, 0:2048]).then_inc(sq.h, 16)
                    sq.n += 32

                    vds_base = sv.n
                    for tch in range(NTC):
                        @block.tensor
                        def _(t, tch=tch, sqn=sq.n, vb=vds_base):
                            t.wait_ge(sq.h, sqn)
                            t.wait_ge(sv.h, vb if tch == 0 else vb + tch)
                            t.matmul(psf[5][:], c1x16[:],
                                     flatv[0:1, tch * 512:(tch + 1) * 512],
                                     start=True, stop=True).then_inc(st_.h, 1)
                        st_.n += 1

                        @block.vector
                        def _(v, tch=tch, plo=plo, stn=st_.n):
                            v.wait_ge(st_.h, stn)
                            v.tensor_mul(vds[plo:plo + 64, tch * 512:(tch + 1) * 512],
                                         vd2[plo:plo + 64, tch * 512:(tch + 1) * 512],
                                         psf[5][plo:plo + 64, :]).then_inc(sv.h, 1)
                        sv.n += 1

                @block.gpsimd
                def _(g, hp=hp):
                    if hp >= 1:
                        g.wait_ge(st_.h, pair_w[hp - 1])
                    for n01 in range(2):
                        n = 2 * hp + n01
                        g.dma_start(out=wvu_sb[n01 * 64:n01 * 64 + 64, :],
                                    in_=wvu_h.ap()[n, :, :]).then_inc(gq.h, 16)
                gq.n += 32

                add_base = sv.n

                @block.tensor
                def _(t, gqn=gq.n, svn=sv.n, ab=add_base):
                    t.wait_ge(gq.h, gqn)
                    t.wait_ge(sv.h, svn)
                    k = 0
                    for ec in range(NDT):
                        for tch in range(NTC):
                            b = 6 + (k % 2)
                            t.wait_ge(sv.h, ab if k < 2 else ab + k - 1)
                            t.matmul(psf[b][:], wvu_sb[:, ec * 128:(ec + 1) * 128],
                                     vds[:, tch * 512:(tch + 1) * 512],
                                     start=True, stop=True).then_inc(st_.h, 1)
                            k += 1
                st_.n += NDT * NTC
                pair_w[hp] = st_.n
                pair_done[hp] = st_.n

                @block.vector
                def _(v, base=st_.n - NDT * NTC):
                    k = 0
                    for ec in range(NDT):
                        for tch in range(NTC):
                            b = 6 + (k % 2)
                            v.wait_ge(st_.h, base + k + 1)
                            v.tensor_add(X["v"][:, ec, tch * 512:(tch + 1) * 512],
                                         X["v"][:, ec, tch * 512:(tch + 1) * 512],
                                         psf[b][:]).then_inc(sv.h, 1)
                            k += 1
                sv.n += NDT * NTC

            barrier()

        def act_view(u):
            if u < 16:
                return B1[:, u * 1024:(u + 1) * 1024]
            if u < 24:
                return B3[:, (u - 16) * 1024:(u - 15) * 1024]
            return B4[:, (u - 24) * 1024:(u - 23) * 1024]

        def phase_moe():
            nbm = {}
            for dt in range(NDT):
                pool = B2 if dt < 4 else B5
                nbm[dt] = pool[:, (dt % 4) * 2048:((dt % 4) + 1) * 2048]

            class Shim:
                def __getitem__(self, key):
                    _, dt, tsl = key
                    return nbm[dt][:, tsl]
            nbw = Shim()
            emit_norm(X["v"], norm_slots(), write_nb=True, dst_view=nbw)
            barrier()

            gw = W5[:, 4096:4160]
            gsl = F3[:, 0:1024]
            crow = vec[0:1, 0:1024]

            @block.gpsimd
            def _(g):
                g.dma_start(out=gw.rearrange("p (dt e) -> p dt e", dt=NDT),
                            in_=inp["moe_gwr"].ap()
                            .rearrange("dt p e -> p dt e")).then_inc(gq.h, 16)
            gq.n += 16
            gw_ld = gq.n

            w1xa = inp["moe_w1x"].ap()
            w1ga = inp["moe_w1g"].ap()
            w2ta = inp["moe_w2t"].ap()

            for th in range(2):
                tho = th * 1024
                for tc in range(2):
                    @block.tensor
                    def _(t, tc=tc, gqn=gw_ld, svn=sv.n):
                        t.wait_ge(gq.h, gqn)
                        t.wait_ge(sv.h, svn)
                        for dt in range(NDT):
                            ins = t.matmul(psf[0][0:8, :], gw[:, dt * 8:(dt + 1) * 8],
                                           nbw[:, dt, tho + tc * 512:tho + (tc + 1) * 512],
                                           start=(dt == 0), stop=(dt == NDT - 1))
                        ins.then_inc(st_.h, 1)
                    st_.n += 1

                    @block.vector
                    def _(v, tc=tc, stn=st_.n):
                        v.wait_ge(st_.h, stn)
                        v.tensor_scalar(F3[0:8, tc * 512:(tc + 1) * 512],
                                        psf[0][0:8, :], gbc[:, 0:1], None,
                                        ALU.add).then_inc(sv.h, 1)
                    sv.n += 1
                logits_done = sv.n

                for tt in range(8):
                    @block.tensor
                    def _(t, tt=tt, ld=logits_done, svn=sv.n):
                        t.wait_ge(sv.h, max(ld, svn))
                        t.transpose(psf[1][:, 0:8],
                                    F3[0:8, tt * 128:(tt + 1) * 128],
                                    cI[0:8, 0:8]).then_inc(st_.h, 1)
                    st_.n += 1

                    def vstep(fn):
                        @block.vector
                        def _(v, svn=sv.n):
                            v.wait_ge(sv.h, svn)
                            fn(v).then_inc(sv.h, 1)
                        sv.n += 1

                    def sbounce():
                        @block.scalar
                        def _(s, svn=sv.n):
                            s.wait_ge(sv.h, svn)
                            s.sem_inc(ss.h, 1)
                        ss.n += 1

                        @block.vector
                        def _(v, ssn=ss.n):
                            v.wait_ge(ss.h, ssn)

                    @block.vector
                    def _(v, stn=st_.n):
                        v.wait_ge(st_.h, stn)
                        v.tensor_copy(sm1[:, 0:8], psf[1][:, 0:8]).then_inc(sv.h, 1)
                    sv.n += 1
                    vstep(lambda v: v.tensor_reduce(sm1[:, 8:9], sm1[:, 0:8],
                                                    AX.X, ALU.max))
                    sbounce()
                    vstep(lambda v: v.tensor_scalar(sm1[:, 9:17], sm1[:, 0:8],
                                                    sm1[:, 8:9], None, ALU.is_equal))
                    vstep(lambda v: v.scalar_tensor_tensor(
                        sm1[:, 17:25], sm1[:, 9:17], -1e30, sm1[:, 0:8],
                        ALU.mult, ALU.add))
                    vstep(lambda v: v.tensor_reduce(sm1[:, 25:26], sm1[:, 17:25],
                                                    AX.X, ALU.max))
                    vstep(lambda v: v.tensor_sub(sm1[:, 34:35], sm1[:, 8:9],
                                                 sm1[:, 25:26]))

                    @block.scalar
                    def _(s, svn=sv.n):
                        s.wait_ge(sv.h, svn)
                        s.activation(sm1[:, 35:36], sm1[:, 34:35],
                                     AF.Sigmoid).then_inc(ss.h, 1)
                    ss.n += 1

                    @block.vector
                    def _(v, ssn=ss.n):
                        v.wait_ge(ss.h, ssn)
                        v.tensor_scalar(sm1[:, 26:34], sm1[:, 17:25], sm1[:, 25:26],
                                        None, ALU.is_equal).then_inc(sv.h, 1)
                    sv.n += 1
                    vstep(lambda v: v.tensor_scalar(sm1[:, 36:37], sm1[:, 35:36],
                                                    -1.0, 1.0, ALU.mult, ALU.add))
                    sbounce()
                    vstep(lambda v: v.tensor_scalar(sm1[:, 0:8], sm1[:, 26:34],
                                                    sm1[:, 36:37], None, ALU.mult))
                    vstep(lambda v: v.scalar_tensor_tensor(
                        sm1[:, 17:25], sm1[:, 9:17], sm1[:, 35:36], sm1[:, 0:8],
                        ALU.mult, ALU.add))

                    @block.tensor
                    def _(t, svn=sv.n):
                        t.wait_ge(sv.h, svn)
                        t.transpose(psf[2][0:8, 0:128], sm1[:, 17:25],
                                    cI[:]).then_inc(st_.h, 1)
                    st_.n += 1

                    @block.vector
                    def _(v, tt=tt, stn=st_.n):
                        v.wait_ge(st_.h, stn)
                        v.tensor_copy(F1[0:8, tt * 128:(tt + 1) * 128],
                                      psf[2][0:8, 0:128]).then_inc(sv.h, 1)
                    sv.n += 1

                @block.sync
                def _(q, svn=sv.n):
                    q.wait_ge(sv.h, svn)
                    q.dma_start(out=coef_hb.ap(),
                                in_=F1[0:8, 0:1024]).then_inc(sq.h, 16)
                sq.n += 16
                coef_ld = sq.n
                barrier()

                if "MDBG" in stages and th == 0:
                    @block.sync
                    def _(q):
                        q.dma_start(out=out_h.ap()[0:8, :],
                                    in_=F3[0:8, 0:1024]).then_inc(sq.h, 16)
                        q.dma_start(out=out_h.ap()[16:24, :],
                                    in_=F1[0:8, 0:1024]).then_inc(sq.h, 16)
                    sq.n += 32

                elist = range(NE)
                if "MDBG" in stages:
                    elist = [0] if th == 0 else []
                for e in elist:
                    prev_st = st_.n
                    @block.sync
                    def _(q, e=e, chb=coef_ld, stn=st_.n):
                        q.wait_ge(sq.h, chb)
                        if e > 0:
                            q.wait_ge(st_.h, stn)
                        q.dma_start(out=crow,
                                    in_=coef_hb.ap()[e:e + 1, :]).then_inc(sq.h, 16)
                    sq.n += 16

                    @block.gpsimd
                    def _(g, e=e, svn=sv.n, ssn=ss.n):
                        if e > 0:
                            g.wait_ge(sv.h, svn)
                            g.wait_ge(ss.h, ssn)
                        g.dma_start(out=b1sb[:],
                                    in_=inp["moe_b1c"].ap()[e]).then_inc(gq.h, 16)
                        g.dma_start(out=b2sb[:],
                                    in_=inp["moe_b2c"].ap()[e]).then_inc(gq.h, 16)
                    gq.n += 32
                    bias_ld = gq.n

                    for tc in range(2):
                        @block.tensor
                        def _(t, tc=tc, sqn=sq.n, svn=sv.n):
                            t.wait_ge(sq.h, sqn)
                            t.wait_ge(sv.h, svn)
                            t.matmul(psf[7][:], c1x[:],
                                     crow[0:1, tc * 512:(tc + 1) * 512],
                                     start=True, stop=True).then_inc(st_.h, 1)
                        st_.n += 1

                        @block.vector
                        def _(v, tc=tc, stn=st_.n):
                            v.wait_ge(st_.h, stn)
                            v.tensor_copy(gsl[:, tc * 512:(tc + 1) * 512],
                                          psf[7][:]).then_inc(sv.h, 1)
                        sv.n += 1
                    g_ready = sv.n

                    if "MDBG" in stages and th == 0 and e == 0:
                        @block.sync
                        def _(q, gr=g_ready):
                            q.wait_ge(sv.h, gr)
                            q.dma_start(out=out_h.ap()[32:33, :],
                                        in_=crow).then_inc(sq.h, 16)
                            q.dma_start(out=out_h.ap()[48:176, :],
                                        in_=gsl[:, 0:1024]).then_inc(sq.h, 16)
                        sq.n += 32

                    load_mk = {}
                    evac_sv = {}
                    evac_ss = {}
                    for u in range(32):
                        slot = (u % 2)
                        wx = W5[:, slot * 2048:slot * 2048 + 1024]
                        wg = W5[:, slot * 2048 + 1024:slot * 2048 + 2048]

                        @block.gpsimd
                        def _(g, u=u, e=e, wx=wx, wg=wg, pst=prev_st):
                            if u >= 2:
                                g.wait_ge(st_.h, load_mk[u - 2])
                            else:
                                g.wait_ge(st_.h, pst)
                            g.dma_start(
                                out=wx.rearrange("p (dt f) -> p dt f", dt=NDT),
                                in_=w1xa[e, u]).then_inc(gq.h, 16)
                            g.dma_start(
                                out=wg.rearrange("p (dt f) -> p dt f", dt=NDT),
                                in_=w1ga[e, u]).then_inc(gq.h, 16)
                        gq.n += 32

                        if ("MDBG" in stages and th == 0 and e == 0
                                and u == 31):
                            @block.vector
                            def _(v, svn=sv.n):
                                v.wait_ge(sv.h, svn)
                                v.tensor_copy(act_view(28)[:, 0:512], psf[4][:])
                                v.tensor_copy(act_view(28)[:, 512:1024],
                                              psf[5][:]).then_inc(sv.h, 1)
                            sv.n += 1

                        bb = (u % 2) * 4
                        gr_w = g_ready
                        if "MDBG" in stages and th == 0 and e == 0 and u == 31:
                            gr_w = sv.n
                        @block.tensor
                        def _(t, u=u, wx=wx, wg=wg, bb=bb, gqn=gq.n, gr=gr_w):
                            t.wait_ge(gq.h, gqn)
                            t.wait_ge(sv.h, gr)
                            if u >= 2:
                                t.wait_ge(sv.h, evac_sv[u - 2])
                                t.wait_ge(ss.h, evac_ss[u - 2])
                            for tc in range(2):
                                for dt in range(NDT):
                                    t.matmul(psf[bb + tc][:],
                                             wx[:, dt * 128:(dt + 1) * 128],
                                             nbw[:, dt, tho + tc * 512:tho + (tc + 1) * 512],
                                             start=(dt == 0), stop=(dt == NDT - 1))
                                for dt in range(NDT):
                                    ins = t.matmul(psf[bb + 2 + tc][:],
                                                   wg[:, dt * 128:(dt + 1) * 128],
                                                   nbw[:, dt, tho + tc * 512:tho + (tc + 1) * 512],
                                                   start=(dt == 0), stop=(dt == NDT - 1))
                            ins.then_inc(st_.h, 1)
                        st_.n += 1
                        load_mk[u] = st_.n

                        @block.scalar
                        def _(s, u=u, bb=bb, stn=st_.n, bld=bias_ld, psv=evac_sv):
                            s.wait_ge(st_.h, stn)
                            s.wait_ge(gq.h, bld)
                            if u >= 1:
                                s.wait_ge(sv.h, psv[u - 1])
                            for tc in range(2):
                                ins = s.activation(
                                    W5[:, 4160 + tc * 512:4160 + (tc + 1) * 512],
                                    psf[bb + 2 + tc][:], AF.Silu,
                                    bias=b1sb[:, 32 + u:33 + u])
                            ins.then_inc(ss.h, 1)
                        ss.n += 1
                        evac_ss[u] = ss.n

                        @block.vector
                        def _(v, u=u, bb=bb, ssn=ss.n, bld=bias_ld):
                            v.wait_ge(ss.h, ssn)
                            v.wait_ge(gq.h, bld)
                            for tc in range(2):
                                v.scalar_tensor_tensor(
                                    act_view(u)[:, tc * 512:(tc + 1) * 512],
                                    psf[bb + tc][:], b1sb[:, u:u + 1],
                                    W5[:, 4160 + tc * 512:4160 + (tc + 1) * 512],
                                    ALU.add, ALU.mult).then_inc(sv.h, 1)
                        sv.n += 2
                        evac_sv[u] = sv.n
                    h_done_sv = sv.n
                    h_done_st = st_.n

                    if "MDBG" in stages and th == 0 and e == 0:
                        @block.gpsimd
                        def _(g, hd=h_done_sv):
                            g.wait_ge(sv.h, hd)
                            g.dma_start(out=out_h.ap()[200:328, :],
                                        in_=act_view(0)).then_inc(gq.h, 16)
                            g.dma_start(out=out_h.ap()[340:468, :],
                                        in_=act_view(31)).then_inc(gq.h, 16)
                            g.dma_start(out=out_h.ap()[480:608, :],
                                        in_=W5[:, 2048:3072]).then_inc(gq.h, 16)
                            g.dma_start(out=out_h.ap()[620:748, :],
                                        in_=W5[:, 3072:4096]).then_inc(gq.h, 16)
                            g.dma_start(out=out_h.ap()[760:888, :],
                                        in_=W5[:, 4160:5184]).then_inc(gq.h, 16)
                            g.dma_start(out=out_h.ap()[900:1028, :],
                                        in_=act_view(28)).then_inc(gq.h, 16)
                            g.dma_start(out=out_h.ap()[1040:1168, 0:64],
                                        in_=b1sb[:]).then_inc(gq.h, 16)
                        gq.n += 112
                        continue

                    for tc in range(2):
                        prev_vst = st_.n
                        vload = {}
                        for v_ in range(16):
                            slot = v_ % 2
                            wsl = W5[:, slot * 2048:(slot + 1) * 2048]

                            @block.gpsimd
                            def _(g, v_=v_, tc=tc, e=e, wsl=wsl, pvs=prev_vst):
                                if v_ >= 2:
                                    g.wait_ge(st_.h, vload[v_ - 2])
                                else:
                                    g.wait_ge(st_.h, pvs)
                                g.dma_start(
                                    out=wsl.rearrange("p (ft d) -> p ft d", ft=2),
                                    in_=w2ta[e, v_]).then_inc(gq.h, 16)
                            gq.n += 16

                            @block.tensor
                            def _(t, v_=v_, tc=tc, wsl=wsl, gqn=gq.n,
                                  hsv=h_done_sv, svn=sv.n):
                                t.wait_ge(gq.h, gqn)
                                if v_ == 0:
                                    t.wait_ge(sv.h, hsv if tc == 0 else svn)
                                for dt in range(NDT):
                                    for ft in range(2):
                                        ins = t.matmul(
                                            psf[dt][:],
                                            wsl[:, ft * 1024 + dt * 128:ft * 1024 + (dt + 1) * 128],
                                            act_view(2 * v_ + ft)[:, tc * 512:(tc + 1) * 512],
                                            start=(v_ == 0 and ft == 0),
                                            stop=(v_ == 15 and ft == 1))
                                ins.then_inc(st_.h, 1)
                            st_.n += 1
                            vload[v_] = st_.n

                        @block.vector
                        def _(v, tc=tc, th=th, stn=st_.n, bld=bias_ld, svn=sv.n):
                            v.wait_ge(st_.h, stn)
                            v.wait_ge(gq.h, bld)
                            for dt in range(NDT):
                                v.scalar_tensor_tensor(
                                    W5[:, 4160:4672],
                                    psf[dt][:], b2sb[:, dt:dt + 1],
                                    gsl[:, tc * 512:(tc + 1) * 512],
                                    ALU.add, ALU.mult).then_inc(sv.h, 1)
                                v.wait_ge(sv.h, svn + 2 * dt + 1)
                                v.tensor_add(
                                    X["v"][:, dt, tho + tc * 512:tho + (tc + 1) * 512],
                                    X["v"][:, dt, tho + tc * 512:tho + (tc + 1) * 512],
                                    W5[:, 4160:4672]).then_inc(sv.h, 1)
                        sv.n += 2 * NDT
                barrier()
            barrier()

        def phase_attn_b():
            emit_norm(X["v"], norm_slots(), write_nb=False)
            aB1 = vec2[0:1, :]
            bB1 = vec[0:1, 2048:4096]
            barrier()

            @block.sync
            def _(q):
                q.dma_start(out=ngc[:], in_=inp["b_ng"].ap()).then_inc(sq.h, 16)
                q.dma_start(out=nbc2[:], in_=inp["b_nb"].ap()).then_inc(sq.h, 16)
                q.dma_start(out=fgc[:], in_=inp["b_fg"].ap()).then_inc(sq.h, 16)
                q.dma_start(out=fbc[:], in_=inp["b_fb"].ap()).then_inc(sq.h, 16)
                q.dma_start(out=f1bsb[:], in_=inp["b_ff1b"].ap()).then_inc(sq.h, 16)
                q.dma_start(out=f2bsb[:], in_=inp["b_ff2b"].ap()).then_inc(sq.h, 16)
            sq.n += 96
            affld = sq.n

            for tch in range(NTC):
                o = tch * 512

                @block.tensor
                def _(t, o=o, svn=sv.n):
                    t.wait_ge(sv.h, svn)
                    t.matmul(psf[6][:], c1x[:], aB1[0:1, o:o + 512],
                             start=True, stop=True)
                    t.matmul(psf[7][:], c1x[:], bB1[0:1, o:o + 512],
                             start=True, stop=True).then_inc(st_.h, 1)
                st_.n += 1
                bc_done = st_.n

                for dt in range(NDT):
                    @block.vector
                    def _(v, dt=dt, o=o, bd=bc_done, svn=sv.n, ssn=ss.n):
                        v.wait_ge(st_.h, bd)
                        v.wait_ge(ss.h, ssn)
                        v.tensor_mul(F1[:, 0:512], X["v"][:, dt, o:o + 512],
                                     psf[6][:]).then_inc(sv.h, 1)
                        v.wait_ge(sv.h, svn + 1)
                        v.tensor_add(F1[:, 0:512], F1[:, 0:512],
                                     psf[7][:]).then_inc(sv.h, 1)
                    sv.n += 2

                    @block.scalar
                    def _(s, svn=sv.n):
                        s.wait_ge(sv.h, svn)
                        s.activation(F1[:, 512:1024], F1[:, 0:512],
                                     AF.Square).then_inc(ss.h, 1)
                    ss.n += 1

                    @block.tensor
                    def _(t, dt=dt, ssn=ss.n):
                        t.wait_ge(ss.h, ssn)
                        t.matmul(psf[4][0:1, :], cCol[:], F1[:, 0:512],
                                 start=(dt == 0), stop=(dt == NDT - 1))
                        t.matmul(psf[5][0:1, :], cCol[:], F1[:, 512:1024],
                                 start=(dt == 0), stop=(dt == NDT - 1)
                                 ).then_inc(st_.h, 1)
                    st_.n += 1

                mu2 = F1[0:1, 0:512]
                r2 = F1[0:1, 512:1024]

                @block.vector
                def _(v, stn=st_.n, svn=sv.n):
                    v.wait_ge(st_.h, stn)
                    v.tensor_scalar_mul(mu2, psf[4][0:1, :],
                                        1.0 / D).then_inc(sv.h, 1)
                    v.wait_ge(sv.h, svn + 1)
                    v.tensor_mul(r2, mu2, mu2).then_inc(sv.h, 1)
                    v.wait_ge(sv.h, svn + 2)
                    v.scalar_tensor_tensor(r2, psf[5][0:1, :], 1.0 / D, r2,
                                           ALU.mult, ALU.subtract).then_inc(sv.h, 1)
                sv.n += 3

                @block.scalar
                def _(s, svn=sv.n):
                    s.wait_ge(sv.h, svn)
                    s.activation(r2, r2, AF.Ln, bias=cVal[0:1, 0:1])
                    s.activation(r2, r2, AF.Exp, scale=-0.5).then_inc(ss.h, 1)
                ss.n += 1

                @block.vector
                def _(v, o=o, ssn=ss.n, svn=sv.n):
                    v.wait_ge(ss.h, ssn)
                    v.tensor_sub(bB1[0:1, o:o + 512], bB1[0:1, o:o + 512],
                                 mu2).then_inc(sv.h, 1)
                    v.wait_ge(sv.h, svn + 1)
                    v.tensor_mul(bB1[0:1, o:o + 512], bB1[0:1, o:o + 512],
                                 r2).then_inc(sv.h, 1)
                    v.wait_ge(sv.h, svn + 2)
                    v.tensor_mul(aB1[0:1, o:o + 512], aB1[0:1, o:o + 512],
                                 r2).then_inc(sv.h, 1)
                sv.n += 3
            barrier()
            rho = aB1
            sig = bB1

            hh3 = B1[:].rearrange("p (dt t) -> p dt t", dt=NDT)
            for tch in range(NTC):
                o = tch * 512

                @block.tensor
                def _(t, o=o, svn=sv.n):
                    t.wait_ge(sv.h, svn)
                    t.matmul(psf[6][:], c1x[:], rho[0:1, o:o + 512],
                             start=True, stop=True)
                    t.matmul(psf[7][:], c1x[:], sig[0:1, o:o + 512],
                             start=True, stop=True).then_inc(st_.h, 1)
                st_.n += 1

                for dt in range(NDT):
                    @block.vector
                    def _(v, dt=dt, o=o, stn=st_.n, svn=sv.n, affq=affld):
                        v.wait_ge(st_.h, stn)
                        v.wait_ge(sq.h, affq)
                        v.tensor_mul(F1[:, 0:512], X["v"][:, dt, o:o + 512],
                                     psf[6][:])
                        v.tensor_scalar(F3[:, 0:512], psf[7][:],
                                        ngc[:, dt:dt + 1], nbc2[:, dt:dt + 1],
                                        ALU.mult, ALU.add).then_inc(sv.h, 1)
                        v.wait_ge(sv.h, svn + 1)
                        v.scalar_tensor_tensor(hh3[:, dt, o:o + 512],
                                               F1[:, 0:512], ngc[:, dt:dt + 1],
                                               F3[:, 0:512], ALU.mult,
                                               ALU.add).then_inc(sv.h, 1)
                    sv.n += 2
            barrier()

            for half in range(2):
                @block.sync
                def _(q, half=half, svn=sv.n):
                    q.wait_ge(sv.h, svn)
                    q.dma_start(out=xspill.ap()[:, half * 8192:(half + 1) * 8192],
                                in_=X["sb"][:, half * 8192:(half + 1) * 8192]
                                ).then_inc(sq.h, 16)
                sq.n += 16
            barrier()
            es_x.close()

            es_b = contextlib.ExitStack()

            def sb_b(name, shape, dtype):
                return es_b.enter_context(nc.sbuf_tensor(name, shape, dtype))

            xs = sb_b("xs", [128, 16 * 1028], BF16)
            xs3 = xs[:].rearrange("p (et t) -> p et t", et=16)
            dtv = sb_b("dtv", [128, 8 * 512], F32)
            dtv3 = dtv[:].rearrange("p (eh t) -> p eh t", eh=8)
            xpj = sb_b("xpj", [128, 16 * 96], BF16)
            dtT = sb_b("dtT", [64, 2048], BF16)
            Asb = sb_b("Asb", [128, 256], F32)
            state = sb_b("state", [128, 256], F32)
            cwsb = sb_b("cwsb", [128, 64], F32)
            cbsb = sb_b("cbsb", [128, 16], F32)
            dtbsb = sb_b("dtbsb", [128, 16], F32)
            Dsb = sb_b("Dsb", [128, 16], F32)
            tailsv = sb_b("tailsv", [128, 48], BF16)
            Brow2 = [sb_b("Brow0", [1, 512], BF16), sb_b("Brow1", [1, 512], BF16)]
            Crow2 = [sb_b("Crow0", [1, 512], BF16), sb_b("Crow1", [1, 512], BF16)]

            wpc = [W5[:, 0:2048], W5[:, 2048:4096]]
            hsl = [W5[:, 0:512], W5[:, 512:1024]]
            Bbb2 = [W5[:, 1024:1536], W5[:, 4096:4608]]
            Cbb2 = [W5[:, 1536:2048], W5[:, 4608:5120]]
            Bbb = Bbb2[0]
            Cbb = Cbb2[0]
            dbx = W5[:, 2048:2560]
            psl2 = [W5[:, 3072:3584], W5[:, 2560:3072]]
            dtl_sb = W5[0:64, 2048:2560]
            Bsb = W5[0:16, 3584:4096]
            Csb = W5[32:48, 3584:4096]

            def fwdsl(ot, lo, hi):
                pool = B4 if ot < 4 else B5
                base = (ot % 4) * 2048
                return pool[:, base + lo:base + hi]

            scan_prev = [0, 0]
            da_last = {0: 0, 1: 0}
            pool_last = {0: 0, 1: 0}
            bc_ev = {}

            def mamba(m, is_rev):
                @block.gpsimd
                def _(g, m=m, svn=sv.n, ssn=ss.n, stn=st_.n):
                    g.wait_ge(sv.h, svn)
                    g.wait_ge(ss.h, ssn)
                    g.wait_ge(st_.h, stn)
                    g.dma_start(out=xpj[:].rearrange("p (et c) -> p et c", et=16),
                                in_=inp[f"b_{m}_xprojT"].ap()
                                .rearrange("(et p) c -> p et c", p=128)
                                ).then_inc(gq.h, 16)
                    g.dma_start(out=dtT[:],
                                in_=inp[f"b_{m}_dtT"].ap()).then_inc(gq.h, 16)
                    g.dma_start(out=Asb[:].rearrange("p (et s) -> p et s", et=16),
                                in_=inp[f"b_{m}_Alog"].ap()
                                .rearrange("et p s -> p et s")).then_inc(gq.h, 16)
                    g.dma_start(out=cwsb[:].rearrange("p (et k) -> p et k", et=16),
                                in_=inp[f"b_{m}_cw"].ap()
                                .rearrange("et p k -> p et k")).then_inc(gq.h, 16)
                    g.dma_start(out=cbsb[:], in_=inp[f"b_{m}_cb"].ap()).then_inc(gq.h, 16)
                    g.dma_start(out=dtbsb[:], in_=inp[f"b_{m}_dtb"].ap()).then_inc(gq.h, 16)
                    g.dma_start(out=Dsb[:], in_=inp[f"b_{m}_D"].ap()).then_inc(gq.h, 16)
                gq.n += 112
                smld = gq.n

                @block.scalar
                def _(s, gqn=smld):
                    s.wait_ge(gq.h, gqn)
                    s.activation(Asb[:], Asb[:], AF.Exp).then_inc(ss.h, 1)
                ss.n += 1

                @block.vector
                def _(v, ssn=ss.n):
                    v.wait_ge(ss.h, ssn)
                    v.tensor_scalar_mul(Asb[:], Asb[:], -1.0).then_inc(sv.h, 1)
                sv.n += 1

                @block.gpsimd
                def _(g, svn=sv.n):
                    g.wait_ge(sv.h, svn)
                    g.memset(state[:], 0.0).then_inc(gq.h, 1)
                gq.n += 1

                inw = inp[f"b_{m}_inwt"].ap()
                outw = inp[f"b_{m}_outT"].ap().rearrange("(ct p) o -> ct p o", p=128)

                for c in range(2):
                    j0 = c * 1024

                    @block.vector
                    def _(v, c=c, svn=sv.n, stn=st_.n, gqn=gq.n):
                        v.wait_ge(sv.h, svn)
                        v.wait_ge(st_.h, stn)
                        v.wait_ge(gq.h, gqn)
                        ins = None
                        for et in range(16):
                            if c == 0:
                                ins = v.memset(xs3[:, et, 0:4], 0.0)
                            else:
                                ins = v.tensor_copy(xs3[:, et, 1:4],
                                                    tailsv[:, et * 3:et * 3 + 3])
                        ins.then_inc(sv.h, 1)
                    sv.n += 1
                    hdr_done = sv.n

                    ip_mm = {}
                    ip_ev = {}
                    for ep in range(8):
                        slot = wpc[ep % 2]

                        @block.gpsimd
                        def _(g, ep=ep, slot=slot, stn=st_.n):
                            if ep >= 2:
                                g.wait_ge(st_.h, ip_mm[ep - 2])
                            else:
                                g.wait_ge(st_.h, stn)
                            g.dma_start(
                                out=slot.rearrange("p (dt e) -> p dt e", dt=NDT),
                                in_=inw[ep]).then_inc(gq.h, 16)
                        gq.n += 16

                        @block.tensor
                        def _(t, ep=ep, slot=slot, gqn=gq.n, hd=hdr_done):
                            t.wait_ge(gq.h, gqn)
                            t.wait_ge(sv.h, hd)
                            if ep >= 2:
                                t.wait_ge(sv.h, ip_ev[ep - 2])
                            bb = 4 * (ep % 2)
                            for e2 in range(2):
                                for h2 in range(2):
                                    src = (j0 + h2 * 512 if not is_rev
                                           else 2048 - j0 - (h2 + 1) * 512)
                                    for dt in range(NDT):
                                        ins = t.matmul(
                                            psf[bb + 2 * e2 + h2][:],
                                            slot[:, dt * 256 + e2 * 128:
                                                 dt * 256 + e2 * 128 + 128],
                                            hh3[:, dt, src:src + 512],
                                            start=(dt == 0), stop=(dt == NDT - 1))
                            ins.then_inc(st_.h, 1)
                        st_.n += 1
                        ip_mm[ep] = st_.n

                        @block.vector
                        def _(v, ep=ep, stn=st_.n):
                            v.wait_ge(st_.h, stn)
                            bb = 4 * (ep % 2)
                            ins = None
                            for e2 in range(2):
                                et = 2 * ep + e2
                                for h2 in range(2):
                                    dst = xs3[:, et,
                                              4 + h2 * 512:4 + (h2 + 1) * 512]
                                    if is_rev:
                                        dst = dst[:, ::-1]
                                    ins = v.tensor_copy(dst,
                                                        psf[bb + 2 * e2 + h2][:])
                            ins.then_inc(sv.h, 1)
                        sv.n += 1
                        ip_ev[ep] = sv.n
                    inproj_done = sv.n

                    @block.gpsimd
                    def _(g, ipd=inproj_done):
                        g.wait_ge(sv.h, ipd)
                        ins = None
                        for et in range(16):
                            ins = g.tensor_copy(tailsv[:, et * 3:et * 3 + 3],
                                                xs3[:, et, 1025:1028])
                        ins.then_inc(gq.h, 1)
                    gq.n += 1
                    tail_done = gq.n

                    for et in range(16):
                        @block.vector
                        def _(v, et=et, td=tail_done, svn=sv.n, ssn=ss.n,
                              smq=smld):
                            v.wait_ge(gq.h, max(td, smq))
                            v.wait_ge(ss.h, ssn)
                            v.tensor_scalar(F1[:, 0:1024], xs3[:, et, 1:1025],
                                            cwsb[:, et * 4:et * 4 + 1], None,
                                            ALU.mult).then_inc(sv.h, 1)
                            for k in range(1, 4):
                                v.wait_ge(sv.h, svn + k)
                                v.scalar_tensor_tensor(
                                    F1[:, 0:1024], xs3[:, et, 1 + k:1025 + k],
                                    cwsb[:, et * 4 + k:et * 4 + k + 1],
                                    F1[:, 0:1024], ALU.mult,
                                    ALU.add).then_inc(sv.h, 1)
                        sv.n += 4

                        @block.scalar
                        def _(s, et=et, svn=sv.n):
                            s.wait_ge(sv.h, svn)
                            s.activation(xs3[:, et, 4:1028], F1[:, 0:1024],
                                         AF.Silu,
                                         bias=cbsb[:, et:et + 1]).then_inc(ss.h, 1)
                        ss.n += 1
                    conv_done = ss.n

                    for sb2 in range(2):
                        so = sb2 * 512

                        @block.tensor
                        def _(t, so=so, cd=conv_done, svn=sv.n, ssn=ss.n):
                            t.wait_ge(ss.h, max(cd, ssn))
                            t.wait_ge(sv.h, svn)
                            for et in range(16):
                                st1 = (et == 0)
                                sp1 = (et == 15)
                                t.matmul(psf[0][0:64, :],
                                         xpj[:, et * 96:et * 96 + 64],
                                         xs3[:, et, 4 + so:4 + so + 512],
                                         start=st1, stop=sp1)
                                t.matmul(psf[1][0:16, :],
                                         xpj[:, et * 96 + 64:et * 96 + 80],
                                         xs3[:, et, 4 + so:4 + so + 512],
                                         start=st1, stop=sp1)
                                ins = t.matmul(psf[2][0:16, :],
                                               xpj[:, et * 96 + 80:et * 96 + 96],
                                               xs3[:, et, 4 + so:4 + so + 512],
                                               start=st1, stop=sp1)
                            ins.then_inc(st_.h, 1)
                        st_.n += 1

                        @block.vector
                        def _(v, stn=st_.n):
                            v.wait_ge(st_.h, stn)
                            v.tensor_copy(dtl_sb[:], psf[0][0:64, :])
                            v.tensor_copy(Bsb[:], psf[1][0:16, :])
                            v.tensor_copy(Csb[:],
                                          psf[2][0:16, :]).then_inc(sv.h, 1)
                        sv.n += 1
                        xp_done = sv.n

                        for half in range(2):
                            ets = list(range(half * 8, half * 8 + 8))
                            dtm_ev = {}
                            for et in ets:
                                eh = et % 8

                                @block.tensor
                                def _(t, et=et, xd=xp_done, svn=sv.n):
                                    t.wait_ge(sv.h, max(xd, svn))
                                    t.matmul(psf[3][:],
                                             dtT[:, et * 128:(et + 1) * 128],
                                             dtl_sb[:], start=True,
                                             stop=True).then_inc(st_.h, 1)
                                st_.n += 1

                                @block.vector
                                def _(v, eh=eh, stn=st_.n, smq=smld):
                                    v.wait_ge(st_.h, stn)
                                    v.wait_ge(gq.h, smq)
                                    v.tensor_scalar(dtv3[:, eh, :], psf[3][:],
                                                    dtbsb[:, et:et + 1], None,
                                                    ALU.add).then_inc(sv.h, 1)
                                sv.n += 1
                                dtm_ev[et] = sv.n

                                @block.scalar
                                def _(s, eh=eh, svn=sv.n, ssn=ss.n):
                                    s.wait_ge(sv.h, svn)
                                    s.wait_ge(ss.h, ssn)
                                    s.activation(F3[:, 0:512], dtv3[:, eh, :],
                                                 AF.Abs)
                                    s.activation(F3[:, 0:512], F3[:, 0:512],
                                                 AF.Exp, scale=-1.0)
                                    s.activation(F3[:, 0:512], F3[:, 0:512],
                                                 AF.Ln, bias=1.0).then_inc(ss.h, 1)
                                ss.n += 1

                                @block.vector
                                def _(v, et=et, eh=eh, so=so, ssn=ss.n, svn=sv.n):
                                    v.wait_ge(ss.h, ssn)
                                    v.tensor_scalar_max(F3[:, 512:1024],
                                                        dtv3[:, eh, :],
                                                        0.0).then_inc(sv.h, 1)
                                    v.wait_ge(sv.h, svn + 1)
                                    v.tensor_add(dtv3[:, eh, :], F3[:, 512:1024],
                                                 F3[:, 0:512]).then_inc(sv.h, 1)
                                    v.wait_ge(sv.h, svn + 2)
                                    v.tensor_scalar(B3[:, et * 512:(et + 1) * 512],
                                                    xs3[:, et, 4 + so:4 + so + 512],
                                                    Dsb[:, et:et + 1], None,
                                                    ALU.mult).then_inc(sv.h, 1)
                                    v.wait_ge(sv.h, svn + 3)
                                    v.tensor_mul(xs3[:, et, 4 + so:4 + so + 512],
                                                 dtv3[:, eh, :],
                                                 xs3[:, et, 4 + so:4 + so + 512]
                                                 ).then_inc(sv.h, 1)
                                sv.n += 4
                            dt_half_done = sv.n

                            sdone = {}
                            row_mm = {}
                            for s16 in range(16):
                                par = s16 % 2
                                Bq, Cq = Bbb2[par], Cbb2[par]
                                Br, Cr = Brow2[par], Crow2[par]

                                @block.sync
                                def _(q, s16=s16, xd=xp_done, Br=Br, Cr=Cr):
                                    q.wait_ge(sv.h, xd)
                                    if s16 >= 2:
                                        q.wait_ge(st_.h, row_mm[s16 - 2])
                                    elif scan_prev[0]:
                                        q.wait_ge(st_.h, scan_prev[0])
                                    q.dma_start(out=Br[:],
                                                in_=Bsb[s16:s16 + 1, :]
                                                ).then_inc(sq.h, 16)
                                    q.dma_start(out=Cr[:],
                                                in_=Csb[s16:s16 + 1, :]
                                                ).then_inc(sq.h, 16)
                                sq.n += 32

                                @block.tensor
                                def _(t, s16=s16, sqn=sq.n, dtd=dt_half_done,
                                      Br=Br, Cr=Cr, ssn=ss.n):
                                    t.wait_ge(sq.h, sqn)
                                    t.wait_ge(sv.h, dtd)
                                    if s16 >= 2:
                                        t.wait_ge(sv.h, bc_ev[s16 - 2])
                                    else:
                                        t.wait_ge(ss.h, ssn)
                                    t.matmul(psf[4][:], c1xbf[:], Br[:],
                                             start=True, stop=True)
                                    t.matmul(psf[5][:], c1xbf[:], Cr[:],
                                             start=True,
                                             stop=True).then_inc(st_.h, 1)
                                st_.n += 1
                                row_mm[s16] = st_.n

                                @block.vector
                                def _(v, s16=s16, stn=st_.n, Bq=Bq, Cq=Cq):
                                    v.wait_ge(st_.h, stn)
                                    if s16 >= 2:
                                        v.wait_ge(sv.h, sdone[s16 - 2])
                                    elif scan_prev[1]:
                                        v.wait_ge(sv.h, scan_prev[1])
                                    v.tensor_copy(Bq[:], psf[4][:])
                                    v.tensor_copy(Cq[:],
                                                  psf[5][:]).then_inc(sv.h, 1)
                                sv.n += 1
                                bc_ev[s16] = sv.n
                                bc_ready = sv.n

                                for et in ets:
                                    eh = et % 8
                                    dslot = (et % 2)
                                    da = F3[:, dslot * 512:dslot * 512 + 512]

                                    @block.scalar
                                    def _(s, et=et, eh=eh, s16=s16, da=da,
                                          dslot=dslot):
                                        if da_last[dslot]:
                                            s.wait_ge(sv.h, da_last[dslot])
                                        s.activation(da, dtv3[:, eh, :], AF.Exp,
                                                     scale=Asb[:, et * 16 + s16:
                                                               et * 16 + s16 + 1]
                                                     ).then_inc(ss.h, 1)
                                    ss.n += 1

                                    @block.vector
                                    def _(v, et=et, so=so, s16=s16, da=da,
                                          Bq=Bq, Cq=Cq, ssn=ss.n, svn=sv.n,
                                          gqn=pool_last[et % 2], stn=st_.n):
                                        v.wait_ge(ss.h, ssn)
                                        v.wait_ge(gq.h, gqn)
                                        v.wait_ge(st_.h, stn)
                                        v.tensor_mul(dbx[:],
                                                     xs3[:, et, 4 + so:4 + so + 512],
                                                     Bq[:]).then_inc(sv.h, 1)
                                        v.wait_ge(sv.h, svn + 1)
                                        v.tensor_tensor_scan(
                                            hsl[et % 2][:], da, dbx[:],
                                            state[:, et * 16 + s16:
                                                  et * 16 + s16 + 1],
                                            ALU.mult, ALU.add).then_inc(sv.h, 1)
                                        v.wait_ge(sv.h, svn + 2)
                                        v.tensor_mul(psl2[et % 2][:],
                                                     hsl[et % 2][:],
                                                     Cq[:]).then_inc(sv.h, 1)
                                    sv.n += 3
                                    da_last[dslot] = sv.n

                                    @block.gpsimd
                                    def _(g, et=et, s16=s16, svn=sv.n):
                                        g.wait_ge(sv.h, svn)
                                        g.tensor_copy(
                                            state[:, et * 16 + s16:
                                                  et * 16 + s16 + 1],
                                            hsl[et % 2][:, 511:512])
                                        g.tensor_add(
                                            B3[:, et * 512:(et + 1) * 512],
                                            B3[:, et * 512:(et + 1) * 512],
                                            psl2[et % 2][:]).then_inc(gq.h, 1)
                                    gq.n += 1
                                    pool_last[et % 2] = gq.n
                                sdone[s16] = sv.n
                            scan_prev[0] = st_.n
                            scan_prev[1] = sv.n

                        scan_sv = sv.n
                        scan_gq = gq.n
                        for ep in range(8):
                            slot = wpc[ep % 2]

                            @block.gpsimd
                            def _(g, ep=ep, slot=slot, stn=st_.n,
                                  scv=scan_sv, scg=scan_gq):
                                g.wait_ge(st_.h, stn)
                                g.wait_ge(sv.h, scv)
                                g.wait_ge(gq.h, scg)
                                g.dma_start(
                                    out=slot.rearrange("p (dt e) -> p dt e",
                                                       dt=NDT),
                                    in_=inw[8 + ep]).then_inc(gq.h, 16)
                            gq.n += 16

                            @block.tensor
                            def _(t, ep=ep, so=so, slot=slot, gqn=gq.n,
                                  ssn=ss.n, svn=sv.n):
                                t.wait_ge(gq.h, gqn)
                                t.wait_ge(ss.h, ssn)
                                t.wait_ge(sv.h, svn)
                                src = (j0 + so if not is_rev
                                       else 2048 - j0 - so - 512)
                                for e2 in range(2):
                                    for dt in range(NDT):
                                        ins = t.matmul(
                                            psf[6 + e2][:],
                                            slot[:, dt * 256 + e2 * 128:
                                                 dt * 256 + e2 * 128 + 128],
                                            hh3[:, dt, src:src + 512],
                                            start=(dt == 0), stop=(dt == NDT - 1))
                                ins.then_inc(st_.h, 1)
                            st_.n += 1

                            if not is_rev:
                                @block.scalar
                                def _(s, ep=ep, stn=st_.n):
                                    s.wait_ge(st_.h, stn)
                                    ins = None
                                    for e2 in range(2):
                                        et = 2 * ep + e2
                                        ins = s.activation(
                                            B2[:, et * 512:(et + 1) * 512],
                                            psf[6 + e2][:], AF.Silu)
                                    ins.then_inc(ss.h, 1)
                                ss.n += 1
                            else:
                                @block.scalar
                                def _(s, ep=ep, stn=st_.n, svn=sv.n):
                                    s.wait_ge(st_.h, stn)
                                    s.wait_ge(sv.h, svn)
                                    s.activation(Bbb[:], psf[6][:], AF.Silu)
                                    s.activation(Cbb[:], psf[7][:],
                                                 AF.Silu).then_inc(ss.h, 1)
                                ss.n += 1

                                @block.vector
                                def _(v, ep=ep, ssn=ss.n):
                                    v.wait_ge(ss.h, ssn)
                                    et = 2 * ep
                                    v.tensor_copy(
                                        B2[:, et * 512:(et + 1) * 512][:, ::-1],
                                        Bbb[:])
                                    v.tensor_copy(
                                        B2[:, (et + 1) * 512:(et + 2) * 512][:, ::-1],
                                        Cbb[:]).then_inc(sv.h, 1)
                                sv.n += 1
                        zdone = (ss.n, sv.n)

                        for et in range(16):
                            @block.vector
                            def _(v, et=et, zd=zdone, gqn=gq.n, svn=sv.n):
                                v.wait_ge(ss.h, zd[0])
                                v.wait_ge(sv.h, max(zd[1], svn))
                                v.wait_ge(gq.h, gqn)
                                v.tensor_mul(B3[:, et * 512:(et + 1) * 512],
                                             B3[:, et * 512:(et + 1) * 512],
                                             B2[:, et * 512:(et + 1) * 512]
                                             ).then_inc(sv.h, 1)
                            sv.n += 1
                        yg_done = sv.n

                        op_mm = {}
                        for ct in range(16):
                            slot = wpc[ct % 2]

                            @block.gpsimd
                            def _(g, ct=ct, slot=slot, stn=st_.n):
                                if ct >= 2:
                                    g.wait_ge(st_.h, op_mm[ct - 2])
                                else:
                                    g.wait_ge(st_.h, stn)
                                g.dma_start(out=slot[:, 0:1024],
                                            in_=outw[ct]).then_inc(gq.h, 16)
                            gq.n += 16

                            @block.tensor
                            def _(t, ct=ct, slot=slot, gqn=gq.n, yd=yg_done):
                                t.wait_ge(gq.h, gqn)
                                t.wait_ge(sv.h, yd)
                                for ot in range(NDT):
                                    ins = t.matmul(
                                        psf[ot][:],
                                        slot[:, ot * 128:(ot + 1) * 128],
                                        B3[:, ct * 512:(ct + 1) * 512],
                                        start=(ct == 0), stop=(ct == 15))
                                ins.then_inc(st_.h, 1)
                            st_.n += 1
                            op_mm[ct] = st_.n

                        @block.vector
                        def _(v, c=c, so=so, stn=st_.n, rev=is_rev):
                            v.wait_ge(st_.h, stn)
                            jlo = c * 1024 + so
                            ins = None
                            for ot in range(NDT):
                                dst = fwdsl(ot, jlo, jlo + 512)
                                if not rev:
                                    ins = v.tensor_copy(dst, psf[ot][:])
                                else:
                                    ins = v.tensor_add(dst, dst, psf[ot][:])
                            ins.then_inc(sv.h, 1)
                        sv.n += 1
                    barrier()

            if "BNOM" not in stages:
                mamba("m1", False)
                if "BNOM2" not in stages:
                    mamba("m2", True)
            barrier()
            es_b.close()

            alloc_x("2")
            for half in range(2):
                @block.sync
                def _(q, half=half):
                    q.dma_start(out=X["sb"][:, half * 8192:(half + 1) * 8192],
                                in_=xspill.ap()[:, half * 8192:(half + 1) * 8192]
                                ).then_inc(sq.h, 16)
                sq.n += 16
            barrier()
            emit_norm(X["v"], norm_slots(), write_nb=False)
            barrier()

            if "BDUMPF" in stages:
                @block.gpsimd
                def _(g):
                    ov = out_h.ap().rearrange("(h g p) d -> h p g d", p=128, g=8)
                    g.dma_start(out=ov[0],
                                in_=B4[:].rearrange("p (g d) -> p g d", g=8)
                                ).then_inc(gq.h, 16)
                    g.dma_start(out=ov[1],
                                in_=B5[:].rearrange("p (g d) -> p g d", g=8)
                                ).then_inc(gq.h, 16)
                gq.n += 32
            if "BNOTAIL" in stages:
                return
            xb3 = B1[:].rearrange("p (dt t) -> p dt t", dt=NDT)
            for tch in range(NTC):
                o = tch * 512

                @block.tensor
                def _(t, o=o, svn=sv.n):
                    t.wait_ge(sv.h, svn)
                    t.matmul(psf[6][:], c1x[:], aB1[0:1, o:o + 512],
                             start=True, stop=True)
                    t.matmul(psf[7][:], c1x[:], bB1[0:1, o:o + 512],
                             start=True, stop=True).then_inc(st_.h, 1)
                st_.n += 1

                for dt in range(NDT):
                    @block.vector
                    def _(v, dt=dt, o=o, stn=st_.n, svn=sv.n, ssn=ss.n):
                        v.wait_ge(st_.h, stn)
                        v.wait_ge(ss.h, ssn)
                        v.tensor_mul(F1[:, 0:512], X["v"][:, dt, o:o + 512],
                                     psf[6][:]).then_inc(sv.h, 1)
                        v.wait_ge(sv.h, svn + 1)
                        v.tensor_add(F1[:, 0:512], F1[:, 0:512],
                                     psf[7][:]).then_inc(sv.h, 1)
                        v.wait_ge(sv.h, svn + 2)
                        v.tensor_add(F3[:, 512:1024], F1[:, 0:512],
                                     fwdsl(dt, o, o + 512)).then_inc(sv.h, 1)
                    sv.n += 3

                    @block.scalar
                    def _(s, svn=sv.n):
                        s.wait_ge(sv.h, svn)
                        s.activation(F1[:, 512:1024], F3[:, 512:1024],
                                     AF.Gelu).then_inc(ss.h, 1)
                    ss.n += 1

                    @block.vector
                    def _(v, dt=dt, o=o, ssn=ss.n, svn=sv.n):
                        v.wait_ge(ss.h, ssn)
                        v.tensor_add(xb3[:, dt, o:o + 512], F1[:, 0:512],
                                     F1[:, 512:1024]).then_inc(sv.h, 1)
                        v.wait_ge(sv.h, svn + 1)
                        v.tensor_add(X["v"][:, dt, o:o + 512],
                                     X["v"][:, dt, o:o + 512],
                                     xb3[:, dt, o:o + 512]).then_inc(sv.h, 1)
                    sv.n += 2
            barrier()

            if "BNOFFN" in stages:
                return
            hfm = {}
            for dt in range(NDT):
                pool = B2 if dt < 4 else B3
                hfm[dt] = pool[:, (dt % 4) * 2048:((dt % 4) + 1) * 2048]

            class HfShim:
                def __getitem__(self, key):
                    _, dt, tsl = key
                    return hfm[dt][:, tsl]
            hfw = HfShim()
            emit_norm(xb3, norm_slots(), write_nb=True, affine=(fgc, fbc),
                      dst_view=hfw, ones_col=cColbf)
            barrier()
            if "BDUMPH" in stages:
                @block.gpsimd
                def _(g):
                    ov = out_h.ap().rearrange("(h g p) d -> h p g d", p=128, g=8)
                    g.dma_start(out=ov[0],
                                in_=B2[:].rearrange("p (g d) -> p g d", g=8)
                                ).then_inc(gq.h, 16)
                    g.dma_start(out=ov[1],
                                in_=B3[:].rearrange("p (g d) -> p g d", g=8)
                                ).then_inc(gq.h, 16)
                gq.n += 32
                return

            for dt in range(NDT):
                @block.vector
                def _(v, dt=dt, sqn=sq.n, svn=sv.n):
                    v.wait_ge(sq.h, sqn)
                    v.wait_ge(sv.h, svn)
                    v.tensor_scalar(X["v"][:, dt, :], X["v"][:, dt, :],
                                    f2bsb[:, dt:dt + 1], None,
                                    ALU.add).then_inc(sv.h, 1)
                sv.n += 1

            ff1a = inp["b_ff1t"].ap()
            ff2a = inp["b_ff2T"].ap().rearrange("(fp p) o -> fp p o", p=128)
            fp_mm = {}
            for fp in range(32):
                slot1 = W5[:, (fp % 2) * 1024:(fp % 2) * 1024 + 1024]
                slot2 = W5[:, 2048 + (fp % 2) * 1024:2048 + (fp % 2) * 1024 + 1024]

                @block.gpsimd
                def _(g, fp=fp, slot1=slot1, slot2=slot2, stn=st_.n):
                    if fp >= 2:
                        g.wait_ge(st_.h, fp_mm[fp - 2])
                    else:
                        g.wait_ge(st_.h, stn)
                    g.dma_start(out=slot1.rearrange("p (dt f) -> p dt f", dt=NDT),
                                in_=ff1a[fp]).then_inc(gq.h, 16)
                    g.dma_start(out=slot2[:, 0:1024],
                                in_=ff2a[fp]).then_inc(gq.h, 16)
                gq.n += 32

                for tc in range(NTC):
                    o = tc * 512

                    @block.tensor
                    def _(t, fp=fp, o=o, slot1=slot1, gqn=gq.n, svn=sv.n):
                        t.wait_ge(gq.h, gqn)
                        t.wait_ge(sv.h, svn)
                        for dt in range(NDT):
                            ins = t.matmul(psf[6][:],
                                           slot1[:, dt * 128:(dt + 1) * 128],
                                           hfw[:, dt, o:o + 512],
                                           start=(dt == 0), stop=(dt == NDT - 1))
                        ins.then_inc(st_.h, 1)
                    st_.n += 1

                    @block.scalar
                    def _(s, fp=fp, stn=st_.n, ssn=ss.n):
                        s.wait_ge(st_.h, stn)
                        s.wait_ge(ss.h, ssn)
                        s.activation(W5[:, 4160:4672], psf[6][:], AF.Gelu,
                                     bias=f1bsb[:, fp:fp + 1]).then_inc(ss.h, 1)
                    ss.n += 1

                    @block.tensor
                    def _(t, fp=fp, slot2=slot2, ssn=ss.n, svn=sv.n):
                        t.wait_ge(ss.h, ssn)
                        t.wait_ge(sv.h, svn)
                        for ot in range(NDT):
                            ins = t.matmul(psf[ot][:],
                                           slot2[:, ot * 128:(ot + 1) * 128],
                                           W5[:, 4160:4672],
                                           start=True, stop=True)
                        ins.then_inc(st_.h, 1)
                    st_.n += 1
                    fp_mm[fp] = st_.n

                    @block.vector
                    def _(v, o=o, stn=st_.n, svn=sv.n):
                        v.wait_ge(st_.h, stn)
                        for ot in range(NDT):
                            v.wait_ge(sv.h, svn + ot)
                            v.tensor_add(X["v"][:, ot, o:o + 512],
                                         X["v"][:, ot, o:o + 512],
                                         psf[ot][:]).then_inc(sv.h, 1)
                    sv.n += NDT
            barrier()

        def phase_store():
            cp_done = {}
            store_at = {}
            for tt in range(16):
                land = F1 if tt % 2 == 0 else F3

                @block.tensor
                def _(t, tt=tt):
                    if tt >= 1:
                        t.wait_ge(sv.h, cp_done[tt - 1])
                    for dt in range(NDT):
                        t.transpose(psf[dt][:, 0:128],
                                    X["v"][:, dt, tt * 128:(tt + 1) * 128],
                                    cI[:]).then_inc(st_.h, 1)
                st_.n += NDT

                @block.vector
                def _(v, tt=tt, land=land, base=st_.n - NDT):
                    if tt >= 2:
                        v.wait_ge(sq.h, store_at[tt - 2])
                    for dt in range(NDT):
                        v.wait_ge(st_.h, base + dt + 1)
                        v.tensor_copy(land[:, dt * 128:(dt + 1) * 128],
                                      psf[dt][:, 0:128]).then_inc(sv.h, 1)
                sv.n += NDT
                cp_done[tt] = sv.n

                @block.sync
                def _(q, tt=tt, land=land, svn=sv.n):
                    q.wait_ge(sv.h, svn)
                    q.dma_start(out=out_h.ap()[tt * 128:(tt + 1) * 128, :],
                                in_=land[:, 0:1024]).then_inc(sq.h, 16)
                sq.n += 16
                store_at[tt] = sq.n

            @block.sync
            def _(q, sqn=sq.n):
                q.wait_ge(sq.h, sqn)

        phase_load()
        if "A" in stages:
            phase_attn_a()
        if "M1" in stages:
            phase_moe()
        if "B" in stages:
            phase_attn_b()
        if "M2" in stages:
            phase_moe()
        if "NOSTORE" in stages:
            @block.sync
            def _(q, sqn=sq.n, gqn=gq.n):
                q.wait_ge(sq.h, sqn)
                q.wait_ge(gq.h, gqn)
        else:
            phase_store()

    es_x.close()
    es.close()
    return nc


def make_in_maps(inputs):
    rn = {k: np.asarray(v) for k, v in inputs.items()}
    x = np.ascontiguousarray(rn["L0_attn_a_inputs"], dtype=np.float32)

    def t(a):
        return np.ascontiguousarray(np.asarray(a).T)

    base = {
        "aWQ": rn["L0_attn_a_WQ"], "aWK": rn["L0_attn_a_WK"],
        "aWVd": rn["L0_attn_a_WV_down"], "aWVu": rn["L0_attn_a_WV_up"],
        "moe_w1x": np.ascontiguousarray(
            rn["L0_ffn_w1"][:, :, :4096].reshape(NE, NDT, 128, 32, 128)
            .transpose(0, 3, 2, 1, 4)),
        "moe_w1g": np.ascontiguousarray(
            rn["L0_ffn_w1"][:, :, 4096:].reshape(NE, NDT, 128, 32, 128)
            .transpose(0, 3, 2, 1, 4)),
        "moe_w2t": np.ascontiguousarray(
            rn["L0_ffn_w2"].reshape(NE, 16, 2, 128, D).transpose(0, 1, 3, 2, 4)),
        "moe_b1c": np.ascontiguousarray(
            rn["L0_ffn_b1"].reshape(NE, 64, 128).transpose(0, 2, 1)),
        "moe_b2c": np.ascontiguousarray(
            rn["L0_ffn_b2"].reshape(NE, 8, 128).transpose(0, 2, 1)),
        "moe_gwr": np.ascontiguousarray(rn["L0_ffn_gate_w"].reshape(NDT, 128, NE)),
        "moe_gbc": np.ascontiguousarray(rn["L0_ffn_gate_b"].reshape(NE, 1)),
        "b_ng": t(rn["L0_attn_b_norm_g"].reshape(NDT, 128)),
        "b_nb": t(rn["L0_attn_b_norm_b"].reshape(NDT, 128)),
        "b_fg": t(rn["L0_attn_b_ffn_g"].reshape(NDT, 128)),
        "b_fb": t(rn["L0_attn_b_ffn_b"].reshape(NDT, 128)),
        "b_ff1t": np.ascontiguousarray(
            t(rn["L0_attn_b_ff1_w"]).reshape(NDT, 128, 32, 128)
            .transpose(2, 1, 0, 3)),
        "b_ff1b": t(rn["L0_attn_b_ff1_b"].reshape(32, 128)),
        "b_ff2T": t(rn["L0_attn_b_ff2_w"]),
        "b_ff2b": t(rn["L0_attn_b_ff2_b"].reshape(NDT, 128)),
    }
    for m in ("m1", "m2"):
        base[f"b_{m}_inwt"] = np.ascontiguousarray(
            t(rn[f"L0_attn_b_{m}_in_w"]).reshape(NDT, 128, 16, 256)
            .transpose(2, 1, 0, 3))
        base[f"b_{m}_outT"] = t(rn[f"L0_attn_b_{m}_out_w"])
        base[f"b_{m}_xprojT"] = t(rn[f"L0_attn_b_{m}_xproj_w"])
        base[f"b_{m}_dtT"] = t(rn[f"L0_attn_b_{m}_dt_w"])
        base[f"b_{m}_cw"] = np.ascontiguousarray(
            rn[f"L0_attn_b_{m}_conv_w"].reshape(NIT, 128, 4))
        base[f"b_{m}_cb"] = t(rn[f"L0_attn_b_{m}_conv_b"].reshape(NIT, 128))
        base[f"b_{m}_dtb"] = t(rn[f"L0_attn_b_{m}_dt_b"].reshape(NIT, 128))
        base[f"b_{m}_Alog"] = np.ascontiguousarray(
            rn[f"L0_attn_b_{m}_Alog"].reshape(NIT, 128, DST))
        base[f"b_{m}_D"] = t(rn[f"L0_attn_b_{m}_D"].reshape(NIT, 128))
    for k, v in host_constants().items():
        base[k] = v
    base = {k: np.ascontiguousarray(v, dtype=np.float32) for k, v in base.items()}
    maps = []
    for i in range(NCORES):
        m = dict(base)
        m["xcore"] = np.ascontiguousarray(x[i])
        maps.append(m)
    return maps


def kernel(**inputs):
    nc = build_program()
    in_maps = make_in_maps(inputs)
    res = run_bass_kernel_spmd(nc, in_maps, core_ids=list(range(NCORES)))
    out = np.stack([np.asarray(r["out"]) for r in res.results], axis=0)
    return out.astype(np.float32)
